# version 03
# baseline (speedup 1.0000x reference)
import numpy as np
import ml_dtypes
from contextlib import ExitStack

import concourse.bass as bass
import concourse.mybir as mybir
import bass_rust as _br
from concourse.bass_utils import run_bass_kernel_spmd


def pair_ap(handle, off_a, off_b, width):
    a = handle[:]
    c = a.copy()
    c.ap = _br.VecI64Pair([list(a.ap[0]), [off_b - off_a, 2], [1, width]])
    c.offset = a.offset + off_a
    return c

F32 = mybir.dt.float32
BF16 = mybir.dt.bfloat16
AX = mybir.AxisListType.X
AXC = mybir.AxisListType.C
OP = mybir.AluOpType
ACT = mybir.ActivationFunctionType

D = 1024
T = 2048
S = 8192
NCH = 8
HP = 8
KCH = 64
QT = 4
I = 2816
ICH = 22
MAGIC = float(np.float32(1.5 * 2 ** 23))
EPS_RMS = 1e-6
EPS_LN = 1e-5
KB = 1024
ARENA_KB = 200
LNS_QT = 63


class Prog:
    def __init__(self, nc):
        self.nc = nc
        self.q = {"sync": [], "tensor": [], "vector": [], "scalar": [], "gpsimd": []}
        self.sem_handles = {}
        self.sem_counts = {}
        self.ctx = ExitStack()

    def sem(self, name):
        if name not in self.sem_handles:
            self.sem_handles[name] = self.ctx.enter_context(self.nc.semaphore(name))
            self.sem_counts[name] = 0
        return self.sem_handles[name]

    def emit(self, eng, fn, wait=(), inc=None, inc_n=1):
        h = [(self.sem(s), v) for (s, v) in wait if v > 0]
        ih = self.sem(inc) if inc else None
        if inc:
            self.sem_counts[inc] += inc_n

        def run(e, h=h, ih=ih, fn=fn, inc_n=inc_n):
            for (sh, v) in h:
                e.wait_ge(sh, v)
            ins = fn(e)
            if ih is not None:
                if ins is None:
                    ins = e.engine_nop()
                ins.then_inc(ih, inc_n)
        self.q[eng].append(run)

    def n(self, name):
        return self.sem_counts.get(name, 0)


class Arena:
    def __init__(self, nc, size_kb):
        self.nc = nc
        self.guard = nc.sbuf_tensor("arena", [128, size_kb * KB], mybir.dt.uint8)
        self.handle = self.guard.__enter__()
        self.base = nc.lookup_mloc(self.handle).addr
        self.names = set()

    def t(self, name, shape, dtype, off_kb):
        assert name not in self.names
        self.names.add(name)
        hnd = self.nc.virtual_sbuf_tensor(name, shape, dtype)
        self.nc._tensor(hnd.name, hnd.shape, hnd.dtype, type="SB")
        mloc = self.nc.lookup_mloc(hnd)
        mloc.allocated = not self.nc.target_bir_lowering
        mloc.addr = self.base + int(off_kb * KB)
        return hnd


def _bf(a):
    return np.ascontiguousarray(a.astype(ml_dtypes.bfloat16))


def build(debug=()):
    nc = bass.Bass("TRN2", target_bir_lowering=False, debug=False)
    p = Prog(nc)

    def din(name, shape, dt=F32):
        return nc.dram_tensor(name, shape, dt, kind="ExternalInput")

    def dint(name, shape, dt=F32):
        return nc.dram_tensor(name, shape, dt)

    xT = din("xT", [D, T])
    wqT = din("wqT", [D, D]); wkT = din("wkT", [D, D])
    wvT = din("wvT", [D, D]); woT = din("woT", [D, D])
    cosT = din("cosT", [128, T]); sinT = din("sinT", [128, T])
    signv = din("signv", [128, 1]); qnv = din("qnv", [128, 1]); knv = din("knv", [128, 1])
    e64 = din("e64", [128, 2], BF16)
    bc2 = din("bc2", [2, 128])
    swapT = din("swapT", [128, 128], BF16)
    ones32 = din("ones32", [128, 32], BF16)
    g128 = din("g128", [128, 128])
    ones1f = din("ones1f", [1, 128])
    ones128f = din("ones128f", [128, 1])
    ones128b = din("ones128b", [128, 1], BF16)
    clipx = din("clipx", [1, 1]); clipd = din("clipd", [1, 1])
    wgTd = din("wgT", [D, I]); wuTd = din("wuT", [D, I]); wdTd = din("wdT", [I, D])
    wgd = din("wg", [I, D]); wud = din("wu", [I, D]); wdd = din("wd", [D, I])

    outT = nc.dram_tensor("outT", [D, T], F32, kind="ExternalOutput")

    ag_in = dint("ag_in", [2, D, T], BF16)
    ag_out = dint("ag_out", [4, 2, D, T], BF16)
    ar1_in = dint("ar1_in", [1, 2]); ar1_out = dint("ar1_out", [1, 2])
    ar2_in = dint("ar2_in", [1, 1]); ar2_out = dint("ar2_out", [1, 1])
    ar3_in = dint("ar3_in", [1, 1]); ar3_out = dint("ar3_out", [1, 1])
    wgi_d = dint("wgi_d", [D, I], BF16)
    wui_d = dint("wui_d", [D, I], BF16)
    wdi_d = dint("wdi_d", [I, D], BF16)
    y_d = dint("y_d", [D, T])
    h_d = dint("h_d", [I, T], F32)

    dbg = {}
    shapes = {"dbg_attn": ([D, T], BF16), "dbg_y": ([D, T], F32),
              "dbg_xq": ([D, T], BF16), "dbg_m": ([1, 2], F32),
              "dbg_h": ([I, T], F32), "dbg_wgint": ([128, NCH * I], BF16),
              "dbg_scales": ([128, 64], F32)}
    for name in debug:
        sh, dt = shapes[name]
        dbg[name] = nc.dram_tensor(name, sh, dt, kind="ExternalOutput")

    ctx = p.ctx
    sb = lambda name, shape, dt=F32: ctx.enter_context(nc.sbuf_tensor(name, shape, dt))

    PC = ctx.enter_context(nc.psum_tensor("PC", [128, 4096], F32))
    PA = PC[:, 0:2048]
    PB = PC[:, 2048:4096]

    signv_s = sb("signv_s", [128, 1])
    qnv_s = sb("qnv_s", [128, 1]); qnv8_s = sb("qnv8_s", [128, 1]); knv_s = sb("knv_s", [128, 1])
    e64_s = sb("e64_s", [128, 2], BF16)
    bc2_s = sb("bc2_s", [2, 128])
    swap_s = sb("swap_s", [128, 128], BF16)
    ones32_s = sb("ones32_s", [128, 32], BF16)
    g128_s = sb("g128_s", [128, 128])
    ones1f_s = sb("ones1f_s", [1, 128])
    ones128f_s = sb("ones128f_s", [128, 1])
    ones128b_s = sb("ones128b_s", [128, 1], BF16)
    clip_s = sb("clip_s", [1, 2])
    bias_s = sb("bias_s", [128, 1])
    norm2_s = sb("norm2_s", [2, 32])
    small_s = sb("small_s", [1, 64])
    small2_s = sb("small2_s", [128, 8])
    sgvec_s = sb("sgvec_s", [128, ICH])
    suvec_s = sb("suvec_s", [128, ICH])
    sdvec_s = sb("sdvec_s", [128, NCH])
    wscg_s = sb("wscg_s", [128, ICH])
    wscu_s = sb("wscu_s", [128, ICH])
    wscd_s = sb("wscd_s", [128, NCH])
    rscg_s = sb("rscg_s", [128, ICH])
    rscu_s = sb("rscu_s", [128, ICH])
    rscd_s = sb("rscd_s", [128, NCH])
    hm88_s = sb("hm88_s", [128, 96])
    rx_s = sb("rx_s", [128, 1])
    rd_s = sb("rd_s", [128, 1])
    sx_s = sb("sx_s", [128, 1])

    A = Arena(nc, ARENA_KB)
    qhat = A.t("qhat", [128, NCH * T], BF16, 0)
    attn_acc = A.t("attn_acc", [128, NCH * T], BF16, 32)
    mrow_s = A.t("mrow_s", [1, T], F32, 64)
    rrow_s = A.t("rrow_s", [1, T], F32, 72)
    vrow_s = A.t("vrow_s", [1, T], F32, 80)
    trow_s = A.t("trow_s", [1, T], F32, 88)
    y_s = A.t("y_s", [128, NCH * T], F32, 96)
    xq_s = A.t("xq_s", [128, NCH * T], BF16, 160)
    x_bf = A.t("x_bf", [128, NCH * T], BF16, 32)
    w_bf = A.t("w_bf", [128, NCH * D], BF16, 64)
    stage = [A.t(f"stage{i}", [128, T], F32, 80 + 8 * i) for i in range(2)]
    kf_c = [A.t(f"kf{i}", [128, T], F32, 96 + 8 * i) for i in range(2)]
    kpre = [A.t(f"kpre{i}", [128, T], BF16, 112 + 4 * i) for i in range(2)]
    sqb = [A.t(f"sqb{i}", [128, T], BF16, 120 + 4 * i) for i in range(2)]
    tmpf = [A.t("tmpf0", [128, T], F32, 128)] * 2
    khc = [A.t(f"khc{i}", [128, T], BF16, 136 + 4 * i) for i in range(2)]
    vtile = [A.t(f"vtile{i}", [128, 512], BF16, 144 + i) for i in range(4)]
    arow = A.t("arow", [2, T], F32, 148)
    arow2 = A.t("arow2", [2, T], F32, 156)
    cos_s = A.t("cos_s", [128, T], F32, 164)
    sins_s = A.t("sins_s", [128, T], F32, 172)
    sinr_s = A.t("sinr_s", [128, T], F32, 180)
    khp = [A.t(f"khp{i}", [128, S], BF16, 64 + 16 * i) for i in range(2)]
    vhp = [A.t(f"vhp{i}", [128, S], BF16, 96 + 16 * i) for i in range(2)]
    Pbuf = A.t("Pbuf", [128, 3072], BF16, 128)
    Pslot = [Pbuf[:, i * 1024:(i + 1) * 1024] for i in range(3)]
    sumsb = A.t("sumsb", [128, 512], F32, 134)
    recb = A.t("recb", [128, 512], F32, 136)
    rbw = A.t("rbw", [128, I], F32, 138)
    wrow = [A.t(f"wrow{i}", [128, I], F32, 149 + 11 * i) for i in range(2)]
    wqst_b = [A.t(f"wqst{i}", [128, I], F32, 171 + 11 * i) for i in range(2)]
    wint = A.t("wint", [128, I], BF16, 193)
    wo_bf = A.t("wo_bf", [128, NCH * D], BF16, 0)
    cstage = [A.t(f"cstage{i}", [128, 1024], F32, 16 + 4 * i) for i in range(2)]
    xres = [A.t(f"xres{i}", [128, 512], F32, 24 + 2 * i) for i in range(2)]
    sqst = [A.t(f"sqst{i}", [128, 512], BF16, 28 + i) for i in range(2)]
    mb_s = A.t("mb_s", [128, T], F32, 0)
    rb_s = A.t("rb_s", [128, T], F32, 8)
    xtmp = A.t("xtmp", [128, T], F32, 16)
    xm8_s = A.t("xm8_s", [128, 8], F32, 24)
    wu_int = A.t("wu_int", [128, NCH * I], BF16, 0)
    silg = [A.t(f"silg{i}", [128, 512], F32, 44 + 2 * i) for i in range(2)]
    hpre = [A.t(f"hpre{i}", [128, 512], F32, 48 + 2 * i) for i in range(2)]
    wg_int = A.t("wg_int", [128, NCH * I], BF16, 96)
    wdstage = [A.t(f"wdstage{i}", [128, 1024], BF16, 140 + 2 * i) for i in range(2)]
    y_ld = A.t("y_ld", [128, NCH * 512], F32, 96)
    z_s = A.t("z_s", [128, NCH * 512], F32, 112)
    sqz = [A.t(f"sqz{i}", [128, 512], BF16, 128 + i) for i in range(2)]
    mbq_s = A.t("mbq_s", [128, 512], F32, 130)
    rqb_s = A.t("rqb_s", [128, 512], F32, 132)
    ttile = A.t("ttile", [128, 512], F32, 144)
    hstage = [A.t(f"hstage{i}", [128, 512], F32, 146 + 2 * i) for i in range(2)]
    hq_t = [A.t(f"hq{i}", [128, 512], BF16, 151 + i) for i in range(2)]
    outst = [A.t(f"outst{i}", [128, 512], F32, 153 + 2 * i) for i in range(2)]

    def ld(e, dst, src):
        return e.dma_start(out=dst, in_=src)

    consts = [(cos_s[:], cosT.ap()), (sinr_s[:], sinT.ap()), (signv_s[:], signv.ap()),
              (qnv_s[:], qnv.ap()), (knv_s[:], knv.ap()), (e64_s[:], e64.ap()),
              (bc2_s[:], bc2.ap()), (swap_s[:], swapT.ap()), (ones32_s[:], ones32.ap()),
              (g128_s[:], g128.ap()), (ones1f_s[:], ones1f.ap()), (ones128f_s[:], ones128f.ap()),
              (ones128b_s[:], ones128b.ap()),
              (small_s[0:1, 0:1], clipx.ap()), (small_s[0:1, 1:2], clipd.ap())]
    for dst, src in consts:
        p.emit("sync", lambda e, d=dst, s=src: ld(e, d, s), inc="c_ld", inc_n=16)
    C_CONST = p.n("c_ld")

    xTv = xT.ap().rearrange("(c p) t -> c p t", p=128)
    for c in range(NCH):
        p.emit("sync", lambda e, c=c: ld(e, stage[c % 2][:], xTv[c]),
               wait=[("c_xcast", c - 1)], inc="c_xld", inc_n=16)
        p.emit("vector", lambda e, c=c: e.tensor_copy(x_bf[:, c * T:(c + 1) * T], stage[c % 2][:]),
               wait=[("c_xld", 16 * (c + 1))], inc="c_xcast")
    C_XBF = p.n("c_xcast")

    def load_weight_bf(dram_t, dst_bf, tag, extra=()):
        wv_ = dram_t.ap().rearrange("(c p) f -> c p f", p=128)
        for c in range(NCH):
            w = [("c_xcast", C_XBF), (tag + "cast", c - 1)] + list(extra)
            p.emit("sync", lambda e, c=c, wv_=wv_: ld(e, stage[c % 2][:, 0:D], wv_[c]),
                   wait=w, inc=tag + "ld", inc_n=16)
            p.emit("vector", lambda e, c=c, dst_bf=dst_bf: e.tensor_copy(
                dst_bf[:, c * D:(c + 1) * D], stage[c % 2][:, 0:D]),
                   wait=[(tag + "ld", 16 * (c + 1))], inc=tag + "cast")
        return p.n(tag + "cast")

    C_WK = load_weight_bf(wkT, w_bf, "wk")

    p.emit("vector", lambda e: e.memset(small_s[0:1, 32:48], 0.0),
           wait=[("c_ld", C_CONST)], inc="c_prep0")
    p.emit("vector", lambda e: e.tensor_scalar(sins_s[:], sinr_s[:], signv_s[:], None, OP.mult),
           wait=[("c_ld", C_CONST)], inc="c_prep0")
    p.emit("vector", lambda e: e.tensor_scalar(qnv8_s[:], qnv_s[:], 0.125, None, OP.mult),
           wait=[("c_ld", C_CONST)], inc="c_prep0")
    p.emit("scalar", lambda e: e.activation(clip_s[0:1, 0:2], small_s[0:1, 0:2], ACT.Exp,
                                            bias=small_s[0:1, 32:33], scale=-1.0),
           wait=[("c_prep0", 3)], inc="csg")
    p.emit("vector", lambda e: e.tensor_scalar(clip_s[0:1, 0:2], clip_s[0:1, 0:2], 1.0, None, OP.add),
           wait=[("csg", 1)], inc="csg")
    p.emit("vector", lambda e: e.reciprocal(clip_s[0:1, 0:2], clip_s[0:1, 0:2]),
           wait=[("csg", 2)], inc="csg")

    def qk_phase(tag, wbuf, wsem, w_cast_count, nvec, out_emit, out_wait, norm_col,
                 first_pa=(), first_pb=()):
        for fo in range(NCH):
            par = fo % 2
            for tt in range(4):
                for ki in range(NCH):
                    w = []
                    if tt == 0 and ki == 0:
                        w = [("c_xcast", C_XBF), (wsem, w_cast_count),
                             (tag + "evac", fo)] + (list(first_pa) if fo == 0 else [])
                    p.emit("tensor", lambda e, fo=fo, tt=tt, ki=ki:
                           e.matmul(PA[:, tt * 512:(tt + 1) * 512],
                                    wbuf[:, ki * D + fo * 128: ki * D + (fo + 1) * 128],
                                    x_bf[:, ki * T + tt * 512: ki * T + (tt + 1) * 512],
                                    start=(ki == 0), stop=(ki == NCH - 1)),
                           wait=w, inc=(tag + "proj") if ki == NCH - 1 else None)
            p.emit("vector", lambda e, par=par: e.tensor_copy(kf_c[par][:], PA[:]),
                   wait=[(tag + "proj", (fo + 1) * 4), (tag + "pre", fo - 1)],
                   inc=tag + "evac")
            p.emit("vector", lambda e, par=par: e.tensor_tensor(sqb[par][:], kf_c[par][:], kf_c[par][:], OP.mult),
                   wait=[(tag + "evac", fo + 1), (tag + "s2n", 3 * (fo - 1) + 2)], inc=tag + "sq")
            for n in range(4):
                w = [(tag + "sq", fo + 1), ("c_ld", C_CONST),
                     (tag + "s2n", 3 * fo), (tag + "r1", fo)]
                if fo == 0:
                    w += list(first_pb)
                p.emit("tensor", lambda e, par=par, n=n: e.matmul(
                    PB[0:2, n * 512:(n + 1) * 512], e64_s[:], sqb[par][:, n * 512:(n + 1) * 512],
                    start=True, stop=True),
                       wait=w, inc=(tag + "ssmm") if n == 3 else None)
            p.emit("vector", lambda e: e.tensor_scalar(arow[:], PB[0:2, :], 1.0 / 64, EPS_RMS, OP.mult, OP.add),
                   wait=[(tag + "ssmm", fo + 1)], inc=tag + "tal")
            p.emit("scalar", lambda e: e.activation(arow2[:], arow[:], ACT.Ln),
                   wait=[(tag + "tal", 3 * fo + 1)], inc=tag + "tal")
            p.emit("scalar", lambda e: e.activation(arow[:], arow2[:], ACT.Exp, scale=-0.5),
                   wait=[(tag + "tal", 3 * fo + 2)], inc=tag + "tal")
            for n in range(4):
                p.emit("tensor", lambda e, n=n: e.matmul(
                    PB[:, n * 512:(n + 1) * 512], bc2_s[:], arow[:, n * 512:(n + 1) * 512],
                    start=True, stop=True),
                       wait=[(tag + "tal", 3 * fo + 3)], inc=(tag + "abc") if n == 3 else None)
            p.emit("vector", lambda e, par=par, nvec=nvec: e.scalar_tensor_tensor(
                kpre[par][:], kf_c[par][:], nvec, PB[:], OP.mult, OP.mult),
                   wait=[(tag + "abc", fo + 1), ("c_prep0", 3)], inc=tag + "pre")
            for n in range(4):
                p.emit("tensor", lambda e, par=par, n=n: e.matmul(
                    PB[:, n * 512:(n + 1) * 512], swap_s[:], kpre[par][:, n * 512:(n + 1) * 512],
                    start=True, stop=True),
                       wait=[(tag + "pre", fo + 1)], inc=(tag + "swap") if n == 3 else None)
            p.emit("vector", lambda e, par=par: e.tensor_tensor(tmpf[par][:], PB[:], sins_s[:], OP.mult),
                   wait=[(tag + "swap", fo + 1), (tag + "rope", fo)], inc=tag + "r1")
            p.emit("vector", lambda e, par=par: e.tensor_tensor(khc[par][:], kpre[par][:], cos_s[:], OP.mult),
                   wait=[(tag + "pre", fo + 1), out_wait(fo - 2)], inc=tag + "r2")
            p.emit("vector", lambda e, par=par: e.tensor_tensor(khc[par][:], khc[par][:], tmpf[par][:], OP.add),
                   wait=[(tag + "r1", fo + 1), (tag + "r2", fo + 1)], inc=tag + "rope")
            p.emit("vector", lambda e, par=par: e.tensor_tensor(sqb[par][:], khc[par][:], khc[par][:], OP.mult),
                   wait=[(tag + "rope", fo + 1)], inc=tag + "s2n")
            for n in range(4):
                p.emit("tensor", lambda e, par=par, n=n: e.matmul(
                    PB[0:2, n * 512:(n + 1) * 512], e64_s[:], sqb[par][:, n * 512:(n + 1) * 512],
                    start=True, stop=True),
                       wait=[(tag + "s2n", 3 * fo + 1), (tag + "r1", fo + 1)],
                       inc=(tag + "s2n") if n == 3 else None)
            p.emit("vector", lambda e, fo=fo: e.reduce_max(
                norm2_s[0:2, norm_col * 8 + fo: norm_col * 8 + fo + 1], PB[0:2, :], axis=AX),
                   wait=[(tag + "s2n", 3 * fo + 2)], inc=tag + "s2n")
            out_emit(fo, par)

    agk = ag_in.ap()

    def k_out(fo, par):
        p.emit("sync", lambda e, fo=fo, par=par: ld(e, agk[0, fo * 128:(fo + 1) * 128, :], khc[par][:]),
               wait=[("krope", fo + 1)], inc="kout", inc_n=16)

    def k_out_wait(fo_prev):
        return ("kout", 16 * (fo_prev + 1))

    def q_out(fo, par):
        p.emit("vector", lambda e, fo=fo, par=par: e.tensor_copy(qhat[:, fo * T:(fo + 1) * T], khc[par][:]),
               wait=[("qrope", fo + 1)], inc="qout")

    def q_out_wait(fo_prev):
        return ("qout", fo_prev + 1)

    qk_phase("k", w_bf, "wkcast", C_WK, knv_s[:], k_out, k_out_wait, 1)
    C_WV = load_weight_bf(wvT, w_bf, "wv", extra=[("kproj", 32)])

    vt_i = 0
    for tt in range(16):
        for fh in range(2):
            sl = vt_i % 4
            for ki in range(NCH):
                w = []
                if ki == 0:
                    w = [("wvcast", C_WV), ("vcp", vt_i - 3)]
                    if vt_i == 0:
                        w.append(("kevac", NCH))
                p.emit("tensor", lambda e, tt=tt, fh=fh, ki=ki, sl=sl:
                       e.matmul(PA[:, sl * 512:(sl + 1) * 512],
                                x_bf[:, ki * T + tt * 128: ki * T + (tt + 1) * 128],
                                w_bf[:, ki * D + fh * 512: ki * D + (fh + 1) * 512],
                                start=(ki == 0), stop=(ki == NCH - 1)),
                       wait=w, inc="vproj" if ki == NCH - 1 else None)
            p.emit("vector", lambda e, sl=sl: e.tensor_copy(vtile[sl][:], PA[:, sl * 512:(sl + 1) * 512]),
                   wait=[("vproj", vt_i + 1), ("vout", 64 * (vt_i - 3))], inc="vcp")
            for hpi in range(4):
                hp = fh * 4 + hpi
                p.emit("sync", lambda e, hp=hp, tt=tt, hpi=hpi, sl=sl: e.dma_start(
                    out=agk[1, hp * 128 + tt * 8: hp * 128 + (tt + 1) * 8, :]
                        .rearrange("r c -> (r c)").rearrange("(t d) -> t d", d=128),
                    in_=vtile[sl][:, hpi * 128:(hpi + 1) * 128]),
                       wait=[("vcp", vt_i + 1)], inc="vout", inc_n=16)
            vt_i += 1

    p.emit("gpsimd", lambda e: e.collective_compute(
        "AllGather", OP.bypass,
        replica_groups=[[0, 1, 2, 3], [4, 5, 6, 7]],
        ins=[ag_in.ap().opt()], outs=[ag_out.ap().opt()]),
           wait=[("kout", 16 * NCH), ("vout", 16 * 128)], inc="ag")

    C_WQ = load_weight_bf(wqT, w_bf, "wq", extra=[("vproj", 32)])
    qk_phase("q", w_bf, "wqcast", C_WQ, qnv8_s[:], q_out, q_out_wait, 0,
             first_pa=[("vcp", 32)], first_pb=[("ks2n", 3 * NCH)])

    p.emit("vector", lambda e: e.reduce_max(norm2_s[0:2, 16:17], norm2_s[0:2, 0:8], axis=AX),
           wait=[("qs2n", 3 * NCH)], inc="b1s")
    p.emit("vector", lambda e: e.reduce_max(norm2_s[0:2, 17:18], norm2_s[0:2, 8:16], axis=AX),
           wait=[("ks2n", 3 * NCH), ("b1s", 1)], inc="b1s")
    p.emit("gpsimd", lambda e: e.tensor_reduce(small_s[0:1, 2:4], norm2_s[0:2, 16:18], AXC, OP.max),
           wait=[("b1s", 2)], inc="b1s")
    p.emit("gpsimd", lambda e: ld(e, ar1_in.ap(), small_s[0:1, 2:4]),
           wait=[("b1s", 3)], inc="b1s", inc_n=16)
    p.emit("gpsimd", lambda e: e.collective_compute(
        "AllReduce", OP.max, replica_groups=[list(range(8))],
        ins=[ar1_in.ap().opt()], outs=[ar1_out.ap().opt()]),
           wait=[("b1s", 19)], inc="b1s")
    p.emit("gpsimd", lambda e: ld(e, small_s[0:1, 4:6], ar1_out.ap()),
           wait=[("b1s", 20)], inc="b1s", inc_n=16)
    p.emit("vector", lambda e: e.tensor_scalar(small_s[0:1, 4:6], small_s[0:1, 4:6], 1e-38, None, OP.add),
           wait=[("b1s", 36)], inc="b1s")
    p.emit("scalar", lambda e: e.activation(small_s[0:1, 6:8], small_s[0:1, 4:6], ACT.Ln),
           wait=[("b1s", 37)], inc="b1s")
    p.emit("vector", lambda e: e.reduce_sum(small_s[0:1, 8:9], small_s[0:1, 6:8], axis=AX),
           wait=[("b1s", 38)], inc="b1s")
    p.emit("scalar", lambda e: e.activation(small_s[0:1, 9:10], small_s[0:1, 8:9], ACT.Exp, scale=0.5),
           wait=[("b1s", 39)], inc="b1s")
    p.emit("vector", lambda e: e.tensor_scalar(small_s[0:1, 10:11], small_s[0:1, 9:10], -1.0, None, OP.mult),
           wait=[("b1s", 40)], inc="b1s")
    p.emit("tensor", lambda e: e.matmul(PB[:, 0:1], ones1f_s[:], small_s[0:1, 10:11], start=True, stop=True),
           wait=[("b1s", 41), ("qs2n", 3 * NCH)], inc="b1s")
    p.emit("vector", lambda e: e.tensor_copy(bias_s[:], PB[:, 0:1]),
           wait=[("b1s", 42)], inc="b1s")
    B1DONE = 43

    if "dbg_m" in dbg:
        p.emit("sync", lambda e: ld(e, dbg["dbg_m"].ap(), small_s[0:1, 4:6]),
               wait=[("b1s", 41)], inc="dbgs", inc_n=16)

    wq_steps = []
    wqc = {}

    def wqn(name):
        return wqc.get(name, 0)

    def wqstep(eng, fn, wait=(), inc=None, inc_n=1):
        wq_steps.append((eng, fn, [(s, v) for (s, v) in wait if v > 0], inc, inc_n))
        if inc:
            wqc[inc] = wqc.get(inc, 0) + inc_n

    def wq_weight(w_rm, wT_d, wi_d, nch_rows, nch_cols, wsc, rsc):
        rows_v = w_rm.ap().rearrange("(c p) d -> c p d", p=128)
        ncols_row = w_rm.shape[1]
        L0 = [wqn("wqldE"), wqn("wqldO")]
        R0 = wqn("wqred"); C0 = wqn("wqrsb")
        T0 = [wqn("wqtlE"), wqn("wqtlO")]
        Q0 = wqn("wqq"); S0_ = wqn("wqst")
        for rc in range(nch_rows):
            lsem = "wqldE" if rc % 2 == 0 else "wqldO"
            wqstep("gpsimd", lambda e, rc=rc: ld(e, wrow[rc % 2][:, 0:ncols_row], rows_v[rc]),
                   [("qs2n", 3 * NCH), ("wqred", R0 + rc - 1)], lsem, 16)
            wqstep("vector", lambda e, rc=rc: e.reduce_max(
                wsc[:, rc:rc + 1], wrow[rc % 2][:, 0:ncols_row],
                axis=AX, apply_absolute_value=True),
                   [(lsem, L0[rc % 2] + 16 * (rc // 2 + 1))], "wqred", 1)
        wqstep("vector", lambda e: e.reciprocal(rsc[:, 0:nch_rows], wsc[:, 0:nch_rows]),
               [("wqred", R0 + nch_rows), ("wqrsb", C0)], "wqrsb", 1)
        wqstep("vector", lambda e: e.tensor_scalar(rsc[:, 0:nch_rows], rsc[:, 0:nch_rows],
                                                   127.0, None, OP.mult),
               [("wqrsb", C0 + 1)], "wqrsb", 1)
        NI_ = nch_rows * 128
        for rc in range(nch_rows):
            wqstep("gpsimd", lambda e, rc=rc: e.dma_start(
                out=rbw[0:1, rc * 128:(rc + 1) * 128],
                in_=rsc[:, rc:rc + 1], allow_slow_non_contiguous=True),
                   [("wqrsb", C0 + 2), ("wqq", Q0)], "wqrsb", 16)
        for pp in range(1, 128):
            wqstep("gpsimd", lambda e, pp=pp, NI_=NI_: e.dma_start(
                out=rbw[pp:pp + 1, 0:NI_], in_=rbw[0:1, 0:NI_]),
                   [("wqrsb", C0 + 2 + 16 * nch_rows)], "wqrsb", 16)
        CB = C0 + 2 + 16 * nch_rows + 127 * 16
        wTv = wT_d.ap().rearrange("(c p) f -> c p f", p=128)
        wiv = wi_d.ap().rearrange("(c p) f -> c p f", p=128)
        for cc in range(nch_cols):
            tsem = "wqtlE" if cc % 2 == 0 else "wqtlO"
            wqstep("gpsimd", lambda e, cc=cc, NI_=NI_: ld(e, wqst_b[cc % 2][:, 0:NI_], wTv[cc]),
                   [("wqrsb", CB), ("wqq", Q0 + 3 * (cc - 1))], tsem, 16)
            wqstep("vector", lambda e, cc=cc, NI_=NI_: e.tensor_tensor(
                wqst_b[cc % 2][:, 0:NI_], wqst_b[cc % 2][:, 0:NI_], rbw[:, 0:NI_], OP.mult),
                   [(tsem, T0[cc % 2] + 16 * (cc // 2 + 1)), ("wqq", Q0 + 3 * cc)], "wqq", 1)
            wqstep("vector", lambda e, cc=cc, NI_=NI_: e.tensor_scalar(
                wqst_b[cc % 2][:, 0:NI_], wqst_b[cc % 2][:, 0:NI_], MAGIC, -MAGIC, OP.add, OP.add),
                   [("wqq", Q0 + 3 * cc + 1)], "wqq", 1)
            wqstep("vector", lambda e, cc=cc, NI_=NI_: e.tensor_scalar(
                wint[:, 0:NI_], wqst_b[cc % 2][:, 0:NI_], 127.0, -128.0, OP.min, OP.max),
                   [("wqq", Q0 + 3 * cc + 2), ("wqst", S0_ + 16 * cc)], "wqq", 1)
            wqstep("gpsimd", lambda e, cc=cc, NI_=NI_: e.dma_start(out=wiv[cc], in_=wint[:, 0:NI_]),
                   [("wqq", Q0 + 3 * cc + 3)], "wqst", 16)

    wq_weight(wgd, wgTd, wgi_d, ICH, NCH, wscg_s, rscg_s)
    wq_weight(wud, wuTd, wui_d, ICH, NCH, wscu_s, rscu_s)
    wq_weight(wdd, wdTd, wdi_d, NCH, ICH, wscd_s, rscd_s)
    N_WQST = wqn("wqst")
    WQ_DONE = [("wqst", N_WQST)]
    wq_pos = [0]

    def emit_wq_slice(frac):
        tgt = int(len(wq_steps) * frac)
        while wq_pos[0] < tgt:
            eng, fn, w, incn, incs = wq_steps[wq_pos[0]]
            p.emit(eng, fn, wait=w, inc=incn, inc_n=incs)
            wq_pos[0] += 1

    slot_ps = [PA[:, 0:1024], PA[:, 1024:2048], PB[:, 0:1024]]
    av_ps = PB[:, 1024:1536]
    sums_ps = PB[:, 1536:2048]
    agv = ag_out.ap()

    for hp in range(HP):
        par = hp % 2
        for src in range(4):
            w = []
            if src == 0:
                w = [("ag", 1), ("qs2n", 3 * NCH), ("qout", NCH)]
                if hp >= 2:
                    w.append(("qte", 4 * 4 * (hp - 1)))
            p.emit("sync", lambda e, hp=hp, src=src, par=par: e.dma_start(
                out=khp[par][:, src * T:(src + 1) * T],
                in_=agv[src, 0, hp * 128:(hp + 1) * 128, :]),
                   wait=w, inc="kvld", inc_n=16)
            p.emit("sync", lambda e, hp=hp, src=src, par=par: e.dma_start(
                out=vhp[par][:, src * T:(src + 1) * T],
                in_=agv[src, 1, hp * 128:(hp + 1) * 128, :]
                    .rearrange("r c -> (r c)").rearrange("(lc p d) -> p lc d", p=128, d=128)),
                   inc="kvld", inc_n=16)
        for qt in range(QT):
            qtg = hp * QT + qt
            E0 = qtg * KCH
            S0 = qtg * 32
            EP0 = qtg * 32
            q_lo = hp * T + qt * 512
            for c in range(KCH):
                g = E0 + c
                sl = c % 3
                w = [("exp", (EP0 + (c - 3) // 2 + 1) if c >= 3 else EP0)]
                if c == 0 and qt == 0:
                    w.append(("kvld", 128 * (hp + 1)))
                if c < 3:
                    w.append(("qte", 4 * qtg))
                p.emit("tensor", lambda e, par=par, c=c, q_lo=q_lo, sl=sl: e.matmul(
                    slot_ps[sl][:, 0:512], khp[par][0:64, c * 128:(c + 1) * 128],
                    qhat[0:64, q_lo:q_lo + 512], start=True, stop=True, tile_position=(0, 0)),
                       wait=w)
                p.emit("tensor", lambda e, par=par, c=c, q_lo=q_lo, sl=sl: e.matmul(
                    slot_ps[sl][:, 512:1024], khp[par][64:128, c * 128:(c + 1) * 128],
                    qhat[64:128, q_lo:q_lo + 512], start=True, stop=True, tile_position=(64, 0)),
                       inc="sc")
                if c >= 1:
                    cc = c - 1
                    ssl = cc % 3
                    w = [("exp", EP0 + cc // 2 + 1)]
                    if cc == 0:
                        w.append(("qte", 4 * qtg))
                    p.emit("tensor", lambda e, par=par, cc=cc, ssl=ssl: e.matmul(
                        av_ps[0:64, :], vhp[par][:, cc * 128: cc * 128 + 64],
                        Pslot[ssl][:, 0:512], start=(cc == 0), stop=False,
                        tile_position=(0, 0)),
                           wait=w)
                    p.emit("tensor", lambda e, par=par, cc=cc, ssl=ssl: e.matmul(
                        av_ps[64:128, :], vhp[par][:, cc * 128 + 64: (cc + 1) * 128],
                        Pslot[ssl][:, 512:1024], start=(cc == 0), stop=False,
                        tile_position=(0, 64)),
                           inc="av")
                if c >= 2 and c % 2 == 0:
                    w = [("exp", EP0 + (c - 2) // 2 + 1)]
                    if c == 2:
                        w.append(("qte", 4 * (qtg - 1) + 1))
                    for t4 in range(4):
                        cj = c - 2 + t4 // 2
                        half = t4 % 2
                        p.emit("tensor", lambda e, cj=cj, half=half, t4=t4: e.matmul(
                            sums_ps[t4 * 32:(t4 + 1) * 32, :], ones32_s[:],
                            Pslot[cj % 3][:, half * 512:(half + 1) * 512],
                            start=(cj < 2), stop=False, tile_position=(0, t4 * 32)),
                               wait=w if t4 == 0 else (), inc="sums" if t4 == 3 else None)
                if c % 2 == 1:
                    w = [("sc", g + 1)]
                    if g == 1:
                        w.append(("b1s", B1DONE))
                    if c >= 3:
                        w += [("av", g - 2), ("sums", S0 + (c - 3) // 2 + 1)]
                    pa_ = 1024 * ((c - 1) % 3)
                    pb_ = 1024 * (c % 3)
                    p.emit("scalar", lambda e, pa_=pa_, pb_=pb_: e.activation(
                        pair_ap(Pbuf, pa_, pb_, 1024), pair_ap(PC, pa_, pb_, 1024),
                        ACT.Exp, bias=bias_s[:], scale=1.0),
                           wait=w, inc="exp")
            cc = KCH - 1
            p.emit("tensor", lambda e, par=par, cc=cc: e.matmul(
                av_ps[0:64, :], vhp[par][:, cc * 128: cc * 128 + 64],
                Pslot[cc % 3][:, 0:512], start=False, stop=True, tile_position=(0, 0)),
                   wait=[("exp", EP0 + 32)])
            p.emit("tensor", lambda e, par=par, cc=cc: e.matmul(
                av_ps[64:128, :], vhp[par][:, cc * 128 + 64: (cc + 1) * 128],
                Pslot[cc % 3][:, 512:1024], start=False, stop=True, tile_position=(0, 64)),
                   inc="av")
            for t4 in range(4):
                cj = KCH - 2 + t4 // 2
                half = t4 % 2
                p.emit("tensor", lambda e, cj=cj, half=half, t4=t4: e.matmul(
                    sums_ps[t4 * 32:(t4 + 1) * 32, :], ones32_s[:],
                    Pslot[cj % 3][:, half * 512:(half + 1) * 512],
                    start=False, stop=True, tile_position=(0, t4 * 32)),
                       wait=[("exp", EP0 + 32)] if t4 == 0 else (),
                       inc="sums" if t4 == 3 else None)
            p.emit("vector", lambda e: e.tensor_copy(sumsb[:], sums_ps),
                   wait=[("sums", S0 + 32), ("qte", 4 * qtg)], inc="qte")
            p.emit("tensor", lambda e: e.matmul(PA[:, 0:512], g128_s[:], sumsb[:],
                                                start=True, stop=True),
                   wait=[("qte", 4 * qtg + 1), ("exp", EP0 + 32)], inc="qte")
            p.emit("vector", lambda e: e.reciprocal(recb[:], PA[:, 0:512]),
                   wait=[("qte", 4 * qtg + 2)], inc="qte")
            p.emit("vector", lambda e, q_lo=q_lo: e.tensor_tensor(
                attn_acc[:, q_lo:q_lo + 512], av_ps, recb[:], OP.mult),
                   wait=[("qte", 4 * qtg + 3), ("av", (qtg + 1) * KCH)], inc="qte")
        emit_wq_slice((hp + 1) / HP)

    N_QTE = 4 * HP * QT

    if "dbg_attn" in dbg:
        da_ = dbg["dbg_attn"].ap().rearrange("(c p) t -> c p t", p=128)
        for c in range(NCH):
            p.emit("sync", lambda e, c=c: ld(e, da_[c], attn_acc[:, c * T:(c + 1) * T]),
                   wait=[("qte", N_QTE)], inc="dbgs", inc_n=16)

    woTv = woT.ap().rearrange("(c p) f -> c p f", p=128)
    for c in range(NCH):
        w = [("wocast", c - 1)]
        if c == 0:
            w += [("qte", N_QTE)] + WQ_DONE
        p.emit("sync", lambda e, c=c: ld(e, cstage[c % 2][:, 0:D], woTv[c]),
               wait=w, inc="wold", inc_n=16)
        p.emit("vector", lambda e, c=c: e.tensor_copy(wo_bf[:, c * D:(c + 1) * D],
                                                      cstage[c % 2][:, 0:D]),
               wait=[("wold", 16 * (c + 1))], inc="wocast")
    C_WO = p.n("wocast")

    xTv2 = xT.ap().rearrange("(c p) t -> c p t", p=128)
    wo_i = 0
    for qt in range(QT):
        for oc in range(NCH):
            sl = wo_i % 4
            for ic in range(NCH):
                w = []
                if ic == 0:
                    w = [("qte", N_QTE), ("wocast", C_WO), ("yev", wo_i - 3)]
                p.emit("tensor", lambda e, oc=oc, qt=qt, ic=ic, sl=sl: e.matmul(
                    PA[:, sl * 512:(sl + 1) * 512],
                    wo_bf[:, ic * D + oc * 128:(ic * D) + (oc + 1) * 128],
                    attn_acc[:, ic * T + qt * 512: ic * T + qt * 512 + 512],
                    start=(ic == 0), stop=(ic == NCH - 1)),
                       wait=w, inc="wop" if ic == NCH - 1 else None)
            p.emit("sync", lambda e, oc=oc, qt=qt, i=wo_i: ld(
                e, xres[i % 2][:], xTv2[oc][:, qt * 512:(qt + 1) * 512]),
                   wait=[("yev", wo_i - 1)], inc="xld", inc_n=16)
            p.emit("vector", lambda e, oc=oc, qt=qt, sl=sl, i=wo_i: e.tensor_tensor(
                y_s[:, oc * T + qt * 512: oc * T + qt * 512 + 512],
                PA[:, sl * 512:(sl + 1) * 512], xres[i % 2][:], OP.add),
                   wait=[("wop", wo_i + 1), ("xld", 16 * (wo_i + 1))], inc="yev")
            p.emit("vector", lambda e, oc=oc, qt=qt, i=wo_i: e.tensor_tensor(
                sqst[i % 2][:], y_s[:, oc * T + qt * 512: oc * T + qt * 512 + 512],
                y_s[:, oc * T + qt * 512: oc * T + qt * 512 + 512], OP.mult),
                   wait=[("yev", wo_i + 1)], inc="sqev")
            w = [("yev", wo_i + 1)]
            if oc == 0:
                w.append(("strs", 7 * (qt - 1) + 1))
            p.emit("tensor", lambda e, oc=oc, qt=qt: e.matmul(
                PB[0:1, 0:512], ones128f_s[:],
                y_s[:, oc * T + qt * 512: oc * T + qt * 512 + 512],
                start=(oc == 0), stop=(oc == NCH - 1)),
                   wait=w, inc="sty" if oc == NCH - 1 else None)
            wq_ = [("sqev", wo_i + 1)]
            if oc == 0:
                wq_.append(("strs", 7 * (qt - 1) + 2))
            p.emit("tensor", lambda e, oc=oc, qt=qt, i=wo_i: e.matmul(
                PB[0:1, 512:1024], ones128b_s[:], sqst[i % 2][:],
                start=(oc == 0), stop=(oc == NCH - 1)),
                   wait=wq_, inc="stq" if oc == NCH - 1 else None)
            wo_i += 1
        qsl = (qt * 512, (qt + 1) * 512)
        p.emit("vector", lambda e, qsl=qsl: e.tensor_scalar(
            mrow_s[0:1, qsl[0]:qsl[1]], PB[0:1, 0:512], 1.0 / D, None, OP.mult),
               wait=[("sty", qt + 1), ("strs", 7 * qt)], inc="strs")
        p.emit("vector", lambda e, qsl=qsl: e.tensor_scalar(
            vrow_s[0:1, qsl[0]:qsl[1]], PB[0:1, 512:1024], 1.0 / D, None, OP.mult),
               wait=[("stq", qt + 1), ("strs", 7 * qt + 1)], inc="strs")
        p.emit("vector", lambda e, qsl=qsl: e.tensor_tensor(
            trow_s[0:1, qsl[0]:qsl[1]], mrow_s[0:1, qsl[0]:qsl[1]],
            mrow_s[0:1, qsl[0]:qsl[1]], OP.mult),
               wait=[("strs", 7 * qt + 2)], inc="strs")
        p.emit("vector", lambda e, qsl=qsl: e.tensor_tensor(
            vrow_s[0:1, qsl[0]:qsl[1]], vrow_s[0:1, qsl[0]:qsl[1]],
            trow_s[0:1, qsl[0]:qsl[1]], OP.subtract),
               wait=[("strs", 7 * qt + 3)], inc="strs")
        p.emit("vector", lambda e, qsl=qsl: e.tensor_scalar(
            vrow_s[0:1, qsl[0]:qsl[1]], vrow_s[0:1, qsl[0]:qsl[1]], EPS_LN, None, OP.add),
               wait=[("strs", 7 * qt + 4)], inc="strs")
        p.emit("scalar", lambda e, qsl=qsl: e.activation(
            trow_s[0:1, qsl[0]:qsl[1]], vrow_s[0:1, qsl[0]:qsl[1]], ACT.Ln),
               wait=[("strs", 7 * qt + 5)], inc="strs")
        p.emit("scalar", lambda e, qsl=qsl: e.activation(
            rrow_s[0:1, qsl[0]:qsl[1]], trow_s[0:1, qsl[0]:qsl[1]], ACT.Exp, scale=-0.5),
               wait=[("strs", 7 * qt + 6)], inc="strs")
    N_STRS = 7 * QT

    for qt in range(QT):
        p.emit("tensor", lambda e, qt=qt: e.matmul(
            PB[:, 512:1024], ones1f_s[:], mrow_s[0:1, qt * 512:(qt + 1) * 512],
            start=True, stop=True),
               wait=[("strs", N_STRS), ("mbs", 4 * qt)], inc="mbs")
        p.emit("vector", lambda e, qt=qt: e.tensor_copy(
            mb_s[:, qt * 512:(qt + 1) * 512], PB[:, 512:1024]),
               wait=[("mbs", 4 * qt + 1)], inc="mbs")
        p.emit("tensor", lambda e, qt=qt: e.matmul(
            PB[:, 512:1024], ones1f_s[:], rrow_s[0:1, qt * 512:(qt + 1) * 512],
            start=True, stop=True),
               wait=[("mbs", 4 * qt + 2)], inc="mbs")
        p.emit("vector", lambda e, qt=qt: e.tensor_copy(
            rb_s[:, qt * 512:(qt + 1) * 512], PB[:, 512:1024]),
               wait=[("mbs", 4 * qt + 3)], inc="mbs")
    for oc in range(NCH):
        p.emit("vector", lambda e, oc=oc: e.tensor_tensor(
            xtmp[:], y_s[:, oc * T:(oc + 1) * T], mb_s[:], OP.subtract),
               wait=[("mbs", 4 * QT), ("yev", QT * NCH), ("xms", 3 * oc)], inc="xms")
        p.emit("vector", lambda e: e.tensor_tensor(xtmp[:], xtmp[:], rb_s[:], OP.mult),
               wait=[("xms", 3 * oc + 1)], inc="xms")
        p.emit("vector", lambda e, oc=oc: e.reduce_max(
            xm8_s[:, oc:oc + 1], xtmp[:], axis=AX, apply_absolute_value=True),
               wait=[("xms", 3 * oc + 2)], inc="xms")
    p.emit("vector", lambda e: e.reduce_max(small2_s[:, 0:1], xm8_s[:, 0:8], axis=AX),
           wait=[("xms", 3 * NCH)], inc="xms")
    XMS_D = 3 * NCH + 1
    p.emit("gpsimd", lambda e: e.tensor_reduce(small_s[0:1, 16:17], small2_s[:, 0:1], AXC, OP.max),
           wait=[("xms", XMS_D)], inc="xss")
    p.emit("gpsimd", lambda e: ld(e, ar2_in.ap(), small_s[0:1, 16:17]),
           wait=[("xss", 1)], inc="xss", inc_n=16)
    p.emit("gpsimd", lambda e: e.collective_compute(
        "AllReduce", OP.max, replica_groups=[list(range(8))],
        ins=[ar2_in.ap().opt()], outs=[ar2_out.ap().opt()]),
           wait=[("xss", 17)], inc="xss")
    p.emit("gpsimd", lambda e: ld(e, small_s[0:1, 17:18], ar2_out.ap()),
           wait=[("xss", 18)], inc="xss", inc_n=16)
    p.emit("vector", lambda e: e.tensor_tensor(small_s[0:1, 18:19], small_s[0:1, 17:18],
                                               clip_s[0:1, 0:1], OP.mult),
           wait=[("xss", 34), ("csg", 3)], inc="xss")
    p.emit("vector", lambda e: e.tensor_scalar(small_s[0:1, 18:19], small_s[0:1, 18:19],
                                               1.0 / 127.0, None, OP.mult),
           wait=[("xss", 35)], inc="xss")
    p.emit("vector", lambda e: e.reciprocal(small_s[0:1, 19:20], small_s[0:1, 18:19]),
           wait=[("xss", 36)], inc="xss")
    p.emit("tensor", lambda e: e.matmul(PB[:, 0:1], ones1f_s[:], small_s[0:1, 18:19],
                                        start=True, stop=True),
           wait=[("xss", 37), ("strs", N_STRS)], inc="xss")
    p.emit("vector", lambda e: e.tensor_copy(sx_s[:], PB[:, 0:1]),
           wait=[("xss", 38)], inc="xss")
    p.emit("tensor", lambda e: e.matmul(PB[:, 0:1], ones1f_s[:], small_s[0:1, 19:20],
                                        start=True, stop=True),
           wait=[("xss", 39)], inc="xss")
    p.emit("vector", lambda e: e.tensor_copy(rx_s[:], PB[:, 0:1]),
           wait=[("xss", 40)], inc="xss")
    p.emit("vector", lambda e: e.tensor_scalar(rb_s[:], rb_s[:], rx_s[:], None, OP.mult),
           wait=[("xss", 41), ("xms", XMS_D)], inc="xss")
    XSS_D = 42
    for oc in range(NCH):
        p.emit("vector", lambda e, oc=oc: e.tensor_tensor(
            xtmp[:], y_s[:, oc * T:(oc + 1) * T], mb_s[:], OP.subtract),
               wait=[("xss", XSS_D), ("xqs", 4 * oc)], inc="xqs")
        p.emit("vector", lambda e: e.tensor_tensor(xtmp[:], xtmp[:], rb_s[:], OP.mult),
               wait=[("xqs", 4 * oc + 1)], inc="xqs")
        p.emit("vector", lambda e: e.tensor_scalar(xtmp[:], xtmp[:], MAGIC, -MAGIC, OP.add, OP.add),
               wait=[("xqs", 4 * oc + 2)], inc="xqs")
        p.emit("vector", lambda e, oc=oc: e.tensor_scalar(
            xq_s[:, oc * T:(oc + 1) * T], xtmp[:], 127.0, -128.0, OP.min, OP.max),
               wait=[("xqs", 4 * oc + 3)], inc="xqs")
    N_XQS = 4 * NCH
    ydv = y_d.ap().rearrange("(c p) t -> c p t", p=128)
    for oc in range(NCH):
        p.emit("sync", lambda e, oc=oc: ld(e, ydv[oc], y_s[:, oc * T:(oc + 1) * T]),
               wait=[("xqs", N_XQS)], inc="yspill", inc_n=16)
    N_YSPILL = p.n("yspill")

    if "dbg_y" in dbg:
        dy_ = dbg["dbg_y"].ap().rearrange("(c p) t -> c p t", p=128)
        for c in range(NCH):
            p.emit("sync", lambda e, c=c: ld(e, dy_[c], y_s[:, c * T:(c + 1) * T]),
                   wait=[("yev", QT * NCH), ("dbgs", p.n("dbgs"))], inc="dbgs", inc_n=16)
    if "dbg_xq" in dbg:
        dxq_ = dbg["dbg_xq"].ap().rearrange("(c p) t -> c p t", p=128)
        for c in range(NCH):
            p.emit("sync", lambda e, c=c: ld(e, dxq_[c], xq_s[:, c * T:(c + 1) * T]),
                   wait=[("xqs", N_XQS), ("dbgs", p.n("dbgs"))], inc="dbgs", inc_n=16)

    p.emit("vector", lambda e: e.tensor_scalar(sgvec_s[:, 0:ICH], wscg_s[:, 0:ICH],
                                               sx_s[:], 1.0 / 127.0, OP.mult, OP.mult),
           wait=[("xss", 39)] + WQ_DONE, inc="sgv")
    p.emit("vector", lambda e: e.tensor_scalar(suvec_s[:, 0:ICH], wscu_s[:, 0:ICH],
                                               sx_s[:], 1.0 / 127.0, OP.mult, OP.mult),
           wait=[("sgv", 1)], inc="sgv")
    wgiv = wgi_d.ap().rearrange("(c p) f -> c p f", p=128)
    wuiv = wui_d.ap().rearrange("(c p) f -> c p f", p=128)
    for c in range(NCH):
        w = [("yspill", N_YSPILL), ("wqst", 16 * NCH)] if c == 0 else []
        p.emit("sync", lambda e, c=c: ld(e, wg_int[:, c * I:(c + 1) * I], wgiv[c]),
               wait=w, inc="wgild", inc_n=16)
        w2 = [("xqs", N_XQS), ("wqst", 16 * 2 * NCH)] if c == 0 else []
        p.emit("sync", lambda e, c=c: ld(e, wu_int[:, c * I:(c + 1) * I], wuiv[c]),
               wait=w2, inc="wgild", inc_n=16)
    C_WGI = p.n("wgild")

    hdv = h_d.ap()
    gu_i = 0
    for qt in range(QT):
        for ic in range(ICH):
            j = gu_i % 2
            gps = [PA[:, 0:512], PB[:, 0:512]][j]
            ups = [PA[:, 512:1024], PB[:, 512:1024]][j]
            for dk in range(NCH):
                w = []
                if dk == 0:
                    w = [("wgild", C_WGI), ("sih", 3 * (gu_i - 2) + 2)]
                    if gu_i <= 1:
                        w.append(("xss", XSS_D))
                p.emit("tensor", lambda e, qt=qt, ic=ic, dk=dk, gps=gps: e.matmul(
                    gps, wg_int[:, dk * I + ic * 128: dk * I + (ic + 1) * 128],
                    xq_s[:, dk * T + qt * 512: dk * T + (qt + 1) * 512],
                    start=(dk == 0), stop=(dk == NCH - 1)),
                       wait=w)
                p.emit("tensor", lambda e, qt=qt, ic=ic, dk=dk, ups=ups: e.matmul(
                    ups, wu_int[:, dk * I + ic * 128: dk * I + (ic + 1) * 128],
                    xq_s[:, dk * T + qt * 512: dk * T + (qt + 1) * 512],
                    start=(dk == 0), stop=(dk == NCH - 1)),
                       inc="gup" if dk == NCH - 1 else None)
            p.emit("scalar", lambda e, ic=ic, j=j, gps=gps: e.activation(
                silg[j][:], gps, ACT.Silu, scale=sgvec_s[:, ic:ic + 1]),
                   wait=[("gup", gu_i + 1), ("sgv", 2), ("sih", 3 * (gu_i - 2) + 2)], inc="sih")
            p.emit("vector", lambda e, j=j, ic=ic, ups=ups: e.scalar_tensor_tensor(
                hpre[j][:], ups, suvec_s[:, ic:ic + 1], silg[j][:], OP.mult, OP.mult),
                   wait=[("sih", 3 * gu_i + 1), ("hst", 16 * (gu_i - 1))], inc="sih")
            p.emit("vector", lambda e, j=j, gi=gu_i: e.reduce_max(
                hm88_s[:, gi:gi + 1] if gi < 88 else hm88_s[:, 88:89],
                hpre[j][:], axis=AX, apply_absolute_value=True),
                   wait=[("sih", 3 * gu_i + 2)], inc="sih")
            p.emit("sync", lambda e, qt=qt, ic=ic, j=j: e.dma_start(
                out=hdv[ic * 128:(ic + 1) * 128, qt * 512:(qt + 1) * 512], in_=hpre[j][:]),
                   wait=[("sih", 3 * gu_i + 3)], inc="hst", inc_n=16)
            gu_i += 1
    N_GU = gu_i

    p.emit("vector", lambda e: e.reduce_max(small2_s[:, 1:2], hm88_s[:, 0:88], axis=AX),
           wait=[("sih", 3 * N_GU)], inc="hss")
    p.emit("gpsimd", lambda e: e.tensor_reduce(small_s[0:1, 24:25], small2_s[:, 1:2], AXC, OP.max),
           wait=[("hss", 1)], inc="hss")
    p.emit("gpsimd", lambda e: ld(e, ar3_in.ap(), small_s[0:1, 24:25]),
           wait=[("hss", 2)], inc="hss", inc_n=16)
    p.emit("gpsimd", lambda e: e.collective_compute(
        "AllReduce", OP.max, replica_groups=[list(range(8))],
        ins=[ar3_in.ap().opt()], outs=[ar3_out.ap().opt()]),
           wait=[("hss", 18)], inc="hss")
    p.emit("gpsimd", lambda e: ld(e, small_s[0:1, 25:26], ar3_out.ap()),
           wait=[("hss", 19)], inc="hss", inc_n=16)
    p.emit("vector", lambda e: e.tensor_tensor(small_s[0:1, 26:27], small_s[0:1, 25:26],
                                               clip_s[0:1, 1:2], OP.mult),
           wait=[("hss", 35)], inc="hss")
    p.emit("vector", lambda e: e.tensor_scalar(small_s[0:1, 26:27], small_s[0:1, 26:27],
                                               1.0 / 127.0, None, OP.mult),
           wait=[("hss", 36)], inc="hss")
    p.emit("vector", lambda e: e.reciprocal(small_s[0:1, 27:28], small_s[0:1, 26:27]),
           wait=[("hss", 37)], inc="hss")
    p.emit("tensor", lambda e: e.matmul(PB[:, 0:1], ones1f_s[:], small_s[0:1, 26:27],
                                        start=True, stop=True),
           wait=[("hss", 38), ("gup", N_GU)], inc="hss")
    p.emit("vector", lambda e: e.tensor_copy(rd_s[:], PB[:, 0:1]),
           wait=[("hss", 39)], inc="hss")
    p.emit("vector", lambda e: e.tensor_scalar(sdvec_s[:, 0:NCH], wscd_s[:, 0:NCH],
                                               rd_s[:], 1.0 / 127.0, OP.mult, OP.mult),
           wait=[("hss", 40)], inc="hss")
    p.emit("tensor", lambda e: e.matmul(PB[:, 0:1], ones1f_s[:], small_s[0:1, 27:28],
                                        start=True, stop=True),
           wait=[("hss", 41)], inc="hss")
    p.emit("vector", lambda e: e.tensor_copy(rd_s[:], PB[:, 0:1]),
           wait=[("hss", 42)], inc="hss")
    HSS_D = 43

    if "dbg_wgint" in dbg:
        p.emit("sync", lambda e: ld(e, dbg["dbg_wgint"].ap(), wg_int[:]),
               wait=[("wgild", C_WGI), ("dbgs", p.n("dbgs"))], inc="dbgs", inc_n=16)
    if "dbg_scales" in dbg:
        def dump_scales(e):
            return ld(e, dbg["dbg_scales"].ap().rearrange("p (a b) -> p a b", a=2)[:, 0, 0:ICH],
                      wscg_s[:, 0:ICH])
        p.emit("sync", dump_scales, wait=WQ_DONE + [("dbgs", p.n("dbgs"))], inc="dbgs", inc_n=16)
        p.emit("sync", lambda e: e.dma_start(
            out=dbg["dbg_scales"].ap().rearrange("p (a b) -> p a b", a=2)[:, 1, 0:1],
            in_=sx_s[:], allow_slow_non_contiguous=True),
               wait=[("xss", 39), ("dbgs", p.n("dbgs"))], inc="dbgs", inc_n=16)
    if "dbg_h" in dbg:
        dh_ = dbg["dbg_h"].ap().rearrange("(c p) t -> c p t", p=128)
        for c in range(ICH):
            p.emit("sync", lambda e, c=c: ld(e, dh_[c], hdv[c * 128:(c + 1) * 128, :]),
                   wait=[("hst", 16 * N_GU), ("dbgs", p.n("dbgs"))], inc="dbgs", inc_n=16)

    wdiv = wdi_d.ap().rearrange("(c p) f -> c p f", p=128)
    outv = outT.ap().rearrange("(c p) t -> c p t", p=128)
    dn_i = 0
    for qt in range(QT):
        L = LNS_QT * qt
        for oc in range(NCH):
            w = [("lns", LNS_QT * (qt - 1) + 12 + 3 * NCH)] if qt > 0 else []
            if qt == 0 and oc == 0:
                w = [("yspill", N_YSPILL), ("gup", N_GU)]
            p.emit("sync", lambda e, oc=oc, qt=qt: ld(
                e, y_ld[:, oc * 512:(oc + 1) * 512], ydv[oc][:, qt * 512:(qt + 1) * 512]),
                   wait=w, inc="yld", inc_n=16)
        for hc in range(ICH):
            j = dn_i % 2
            w = [("hst", 16 * N_GU)] if dn_i == 0 else [("hqs", 4 * (dn_i - 1))]
            p.emit("sync", lambda e, qt=qt, hc=hc, j=j: ld(
                e, hstage[j][:], hdv[hc * 128:(hc + 1) * 128, qt * 512:(qt + 1) * 512]),
                   wait=w + ([("wqst", N_WQST)] if dn_i == 0 else []), inc="hld", inc_n=16)
            p.emit("sync", lambda e, hc=hc, j=j: ld(e, wdstage[j][:], wdiv[hc]),
                   wait=([("hss", HSS_D), ("wqst", N_WQST)] if dn_i == 0 else []) + [("dnp", dn_i - 1)],
                   inc="wdld", inc_n=16)
            p.emit("vector", lambda e, j=j: e.tensor_scalar(
                ttile[:], hstage[j][:], rd_s[:], None, OP.mult),
                   wait=[("hld", 16 * (dn_i + 1)), ("hss", HSS_D),
                         ("lns", (LNS_QT * (qt - 1) + 43) if qt > 0 else 0),
                         ("hqs", 4 * dn_i)], inc="hqs")
            p.emit("vector", lambda e: e.tensor_scalar(ttile[:], ttile[:], MAGIC, -MAGIC, OP.add, OP.add),
                   wait=[("hqs", 4 * dn_i + 1)], inc="hqs")
            p.emit("vector", lambda e, j=j: e.tensor_scalar(
                hq_t[j][:], ttile[:], 127.0, -128.0, OP.min, OP.max),
                   wait=[("hqs", 4 * dn_i + 2)], inc="hqs")
            for oc in range(NCH):
                ps = (PA if oc < 4 else PB)[:, (oc % 4) * 512:((oc % 4) + 1) * 512]
                w = [("hqs", 4 * dn_i + 3), ("wdld", 16 * (dn_i + 1))]
                if hc == 0 and oc == 0 and qt > 0:
                    w += [("lns", LNS_QT * (qt - 1) + 47)]
                p.emit("tensor", lambda e, oc=oc, j=j, ps=ps, hc=hc: e.matmul(
                    ps, wdstage[j][:, oc * 128:(oc + 1) * 128], hq_t[j][:],
                    start=(hc == 0), stop=(hc == ICH - 1)),
                       wait=w, inc="dnp" if oc == NCH - 1 else None)
            p.emit("vector", lambda e: None, wait=[("dnp", dn_i + 1)], inc="hqs")
            dn_i += 1
        for oc in range(NCH):
            ps = (PA if oc < 4 else PB)[:, (oc % 4) * 512:((oc % 4) + 1) * 512]
            p.emit("vector", lambda e, oc=oc, ps=ps: e.tensor_scalar(
                z_s[:, oc * 512:(oc + 1) * 512], ps, sdvec_s[:, oc:oc + 1], None, OP.mult),
                   wait=[("dnp", (qt + 1) * ICH), ("lns", L + oc)], inc="lns")
        p.emit("tensor", lambda e, qt=qt: e.matmul(
            PB[:, 1536:2048], ones1f_s[:], mrow_s[0:1, qt * 512:(qt + 1) * 512],
            start=True, stop=True),
               wait=[("lns", L + 8)], inc="lns")
        p.emit("vector", lambda e: e.tensor_copy(mbq_s[:], PB[:, 1536:2048]),
               wait=[("lns", L + 9)], inc="lns")
        p.emit("tensor", lambda e, qt=qt: e.matmul(
            PB[:, 1536:2048], ones1f_s[:], rrow_s[0:1, qt * 512:(qt + 1) * 512],
            start=True, stop=True),
               wait=[("lns", L + 10)], inc="lns")
        p.emit("vector", lambda e: e.tensor_copy(rqb_s[:], PB[:, 1536:2048]),
               wait=[("lns", L + 11)], inc="lns")
        for oc in range(NCH):
            p.emit("vector", lambda e, oc=oc: e.tensor_tensor(
                ttile[:], y_ld[:, oc * 512:(oc + 1) * 512], mbq_s[:], OP.subtract),
                   wait=[("lns", L + 12 + 3 * oc), ("yld", 16 * 8 * (qt + 1)),
                         ("hqs", 4 * (qt + 1) * ICH)], inc="lns")
            p.emit("vector", lambda e: e.tensor_tensor(ttile[:], ttile[:], rqb_s[:], OP.mult),
                   wait=[("lns", L + 12 + 3 * oc + 1)], inc="lns")
            p.emit("vector", lambda e, oc=oc: e.tensor_tensor(
                z_s[:, oc * 512:(oc + 1) * 512], z_s[:, oc * 512:(oc + 1) * 512],
                ttile[:], OP.add),
                   wait=[("lns", L + 12 + 3 * oc + 2)], inc="lns")
        for oc in range(NCH):
            p.emit("vector", lambda e, oc=oc, j=oc % 2: e.tensor_tensor(
                sqz[j][:], z_s[:, oc * 512:(oc + 1) * 512],
                z_s[:, oc * 512:(oc + 1) * 512], OP.mult),
                   wait=[("lns", L + 12 + 3 * oc + 3), ("z2qm", 8 * qt + oc - 1)], inc="zsq")
            w = [("lns", L + 12 + 3 * oc + 3)]
            if oc == 0:
                w.append(("lns", (LNS_QT * (qt - 1) + 37) if qt > 0 else 0))
            p.emit("tensor", lambda e, oc=oc: e.matmul(
                PB[0:1, 1024:1536], ones128f_s[:], z_s[:, oc * 512:(oc + 1) * 512],
                start=(oc == 0), stop=(oc == NCH - 1)),
                   wait=w, inc="z2y" if oc == NCH - 1 else None)
            w = [("zsq", 8 * qt + oc + 1)]
            if oc == 0:
                w.append(("lns", (LNS_QT * (qt - 1) + 38) if qt > 0 else 0))
            p.emit("tensor", lambda e, oc=oc, j=oc % 2: e.matmul(
                PA[0:1, 1536:2048], ones128b_s[:], sqz[j][:],
                start=(oc == 0), stop=(oc == NCH - 1)),
                   wait=w, inc="z2qm" if True else None)
        p.emit("vector", lambda e: e.tensor_scalar(mbq_s[0:1, 0:512], PB[0:1, 1024:1536],
                                                   1.0 / D, None, OP.mult),
               wait=[("z2y", qt + 1), ("lns", L + 36)], inc="lns")
        p.emit("vector", lambda e: e.tensor_scalar(rqb_s[0:1, 0:512], PA[0:1, 1536:2048],
                                                   1.0 / D, None, OP.mult),
               wait=[("z2qm", 8 * (qt + 1)), ("lns", L + 37)], inc="lns")
        p.emit("vector", lambda e: e.tensor_tensor(ttile[0:1, 0:512], mbq_s[0:1, 0:512],
                                                   mbq_s[0:1, 0:512], OP.mult),
               wait=[("lns", L + 38)], inc="lns")
        p.emit("vector", lambda e: e.tensor_tensor(rqb_s[0:1, 0:512], rqb_s[0:1, 0:512],
                                                   ttile[0:1, 0:512], OP.subtract),
               wait=[("lns", L + 39)], inc="lns")
        p.emit("vector", lambda e: e.tensor_scalar(rqb_s[0:1, 0:512], rqb_s[0:1, 0:512],
                                                   EPS_LN, None, OP.add),
               wait=[("lns", L + 40)], inc="lns")
        p.emit("scalar", lambda e: e.activation(ttile[0:1, 0:512], rqb_s[0:1, 0:512], ACT.Ln),
               wait=[("lns", L + 41)], inc="lns")
        p.emit("scalar", lambda e: e.activation(rqb_s[0:1, 0:512], ttile[0:1, 0:512],
                                                ACT.Exp, scale=-0.5),
               wait=[("lns", L + 42)], inc="lns")
        p.emit("tensor", lambda e: e.matmul(PB[:, 1024:1536], ones1f_s[:], mbq_s[0:1, 0:512],
                                            start=True, stop=True),
               wait=[("lns", L + 43)], inc="lns")
        p.emit("vector", lambda e: e.tensor_copy(mbq_s[:], PB[:, 1024:1536]),
               wait=[("lns", L + 44)], inc="lns")
        p.emit("tensor", lambda e: e.matmul(PB[:, 1024:1536], ones1f_s[:], rqb_s[0:1, 0:512],
                                            start=True, stop=True),
               wait=[("lns", L + 45)], inc="lns")
        p.emit("vector", lambda e: e.tensor_copy(rqb_s[:], PB[:, 1024:1536]),
               wait=[("lns", L + 46)], inc="lns")
        for oc in range(NCH):
            j = oc % 2
            p.emit("vector", lambda e, oc=oc, j=j: e.tensor_tensor(
                outst[j][:], z_s[:, oc * 512:(oc + 1) * 512], mbq_s[:], OP.subtract),
                   wait=[("lns", L + 47 + 2 * oc), ("outd", 16 * (8 * qt + oc - 1))], inc="lns")
            p.emit("vector", lambda e, oc=oc, j=j: e.tensor_tensor(
                outst[j][:], outst[j][:], rqb_s[:], OP.mult),
                   wait=[("lns", L + 47 + 2 * oc + 1)], inc="lns")
            p.emit("sync", lambda e, oc=oc, qt=qt, j=j: ld(
                e, outv[oc][:, qt * 512:(qt + 1) * 512], outst[j][:]),
                   wait=[("lns", L + 47 + 2 * oc + 2)], inc="outd", inc_n=16)

    tail_waits = [("outd", 16 * 32)]
    if p.n("dbgs"):
        tail_waits.append(("dbgs", p.n("dbgs")))
    p.emit("gpsimd", lambda e: None, wait=tail_waits, inc="tail")

    return nc, p, {}, dbg


def finalize(nc, p):
    with nc.Block() as block:
        @block.sync
        def _(e):
            for fn in p.q["sync"]:
                fn(e)

        @block.tensor
        def _(e):
            for fn in p.q["tensor"]:
                fn(e)

        @block.vector
        def _(e):
            for fn in p.q["vector"]:
                fn(e)

        @block.scalar
        def _(e):
            for fn in p.q["scalar"]:
                fn(e)

        @block.gpsimd
        def _(e):
            for fn in p.q["gpsimd"]:
                fn(e)
    return nc


def shard_inputs(inputs):
    x = np.asarray(inputs["L0_attn_x"], dtype=np.float32)
    cos = np.asarray(inputs["L0_attn_rope_cos"], dtype=np.float32)
    sin = np.asarray(inputs["L0_attn_rope_sin"], dtype=np.float32)
    P = cos.shape[0]
    cos_pad = np.concatenate([np.ones((S - P, 32), np.float32), cos], axis=0)
    sin_pad = np.concatenate([np.zeros((S - P, 32), np.float32), sin], axis=0)
    pair_idx = (np.arange(128) % 64) // 2
    signv = np.where(np.arange(128) % 2 == 0, -1.0, 1.0).astype(np.float32)[:, None]
    qn = np.asarray(inputs["L0_attn_q_norm_w"], np.float32)
    kn = np.asarray(inputs["L0_attn_k_norm_w"], np.float32)
    qnv = np.concatenate([qn, qn])[:, None].astype(np.float32)
    knv = np.concatenate([kn, kn])[:, None].astype(np.float32)
    e64 = np.zeros((128, 2), np.float32); e64[0:64, 0] = 1; e64[64:128, 1] = 1
    bc2 = np.zeros((2, 128), np.float32); bc2[0, 0:64] = 1; bc2[1, 64:128] = 1
    swapT = np.zeros((128, 128), np.float32)
    for i_ in range(128):
        swapT[i_ ^ 1, i_] = 1.0
    g = np.zeros((128, 128), np.float32)
    g[0, 0:64] = 1; g[64, 0:64] = 1; g[32, 64:128] = 1; g[96, 64:128] = 1
    wq = np.asarray(inputs["L0_attn_wq"], np.float32)
    wk = np.asarray(inputs["L0_attn_wk"], np.float32)
    wv = np.asarray(inputs["L0_attn_wv"], np.float32)
    wo = np.asarray(inputs["L0_attn_wo"], np.float32)
    wg = np.asarray(inputs["L0_ffn_w_gate"], np.float32)
    wu = np.asarray(inputs["L0_ffn_w_up"], np.float32)
    wd = np.asarray(inputs["L0_ffn_w_down"], np.float32)
    common = {
        "wqT": np.ascontiguousarray(wq.T), "wkT": np.ascontiguousarray(wk.T),
        "wvT": np.ascontiguousarray(wv.T), "woT": np.ascontiguousarray(wo.T),
        "signv": signv, "qnv": qnv, "knv": knv,
        "e64": _bf(e64), "bc2": bc2, "swapT": _bf(swapT),
        "ones32": _bf(np.ones((128, 32), np.float32)), "g128": g,
        "ones1f": np.ones((1, 128), np.float32),
        "ones128f": np.ones((128, 1), np.float32),
        "ones128b": _bf(np.ones((128, 1), np.float32)),
        "clipx": np.asarray(inputs["L0_ffn_clip_x"], np.float32).reshape(1, 1),
        "clipd": np.asarray(inputs["L0_ffn_clip_down"], np.float32).reshape(1, 1),
        "wgT": np.ascontiguousarray(wg.T), "wuT": np.ascontiguousarray(wu.T),
        "wdT": np.ascontiguousarray(wd.T),
        "wg": wg, "wu": wu, "wd": wd,
    }
    in_maps = []
    for c in range(8):
        b = c // 4
        t0 = (c % 4) * T
        m = dict(common)
        m["xT"] = np.ascontiguousarray(x[b].T[:, t0:t0 + T])
        m["cosT"] = np.ascontiguousarray(cos_pad[t0:t0 + T, pair_idx].T)
        m["sinT"] = np.ascontiguousarray(sin_pad[t0:t0 + T, pair_idx].T)
        in_maps.append(m)
    return in_maps


def kernel(**inputs):
    in_maps = shard_inputs(inputs)
    nc, p, dram, dbg = build()
    finalize(nc, p)
    res = run_bass_kernel_spmd(nc, in_maps, core_ids=list(range(8)))
    out = np.empty((2, S, D), dtype=np.float32)
    for c in range(8):
        b = c // 4
        t0 = (c % 4) * T
        out[b, t0:t0 + T, :] = res.results[c]["outT"].T
    return out


# version 04 (final)
# speedup vs baseline: 1.3132x; 1.3132x over previous
import numpy as np
import ml_dtypes
from contextlib import ExitStack

import concourse.bass as bass
import concourse.mybir as mybir
from concourse.bass_utils import run_bass_kernel_spmd

F32 = mybir.dt.float32
BF16 = mybir.dt.bfloat16
AX = mybir.AxisListType.X
AXC = mybir.AxisListType.C
OP = mybir.AluOpType
ACT = mybir.ActivationFunctionType

D = 1024
T = 2048
S = 8192
NCH = 8
HP = 8
KCH = 64
QT = 4
I = 2816
ICH = 22
MAGIC = float(np.float32(1.5 * 2 ** 23))
EPS_RMS = 1e-6
EPS_LN = 1e-5
KB = 1024
ARENA_KB = 200
LNS_QT = 63


class Prog:
    def __init__(self, nc):
        self.nc = nc
        self.q = {"sync": [], "tensor": [], "vector": [], "scalar": [], "gpsimd": []}
        self.sem_handles = {}
        self.sem_counts = {}
        self.ctx = ExitStack()

    def sem(self, name):
        if name not in self.sem_handles:
            self.sem_handles[name] = self.ctx.enter_context(self.nc.semaphore(name))
            self.sem_counts[name] = 0
        return self.sem_handles[name]

    def emit(self, eng, fn, wait=(), inc=None, inc_n=1):
        h = [(self.sem(s), v) for (s, v) in wait if v > 0]
        ih = self.sem(inc) if inc else None
        if inc:
            self.sem_counts[inc] += inc_n

        def run(e, h=h, ih=ih, fn=fn, inc_n=inc_n):
            for (sh, v) in h:
                e.wait_ge(sh, v)
            ins = fn(e)
            if ih is not None:
                if ins is None:
                    ins = e.engine_nop()
                ins.then_inc(ih, inc_n)
        self.q[eng].append(run)

    def n(self, name):
        return self.sem_counts.get(name, 0)


class Arena:
    def __init__(self, nc, size_kb):
        self.nc = nc
        self.guard = nc.sbuf_tensor("arena", [128, size_kb * KB], mybir.dt.uint8)
        self.handle = self.guard.__enter__()
        self.base = nc.lookup_mloc(self.handle).addr
        self.names = set()

    def t(self, name, shape, dtype, off_kb):
        assert name not in self.names
        self.names.add(name)
        hnd = self.nc.virtual_sbuf_tensor(name, shape, dtype)
        self.nc._tensor(hnd.name, hnd.shape, hnd.dtype, type="SB")
        mloc = self.nc.lookup_mloc(hnd)
        mloc.allocated = not self.nc.target_bir_lowering
        mloc.addr = self.base + int(off_kb * KB)
        return hnd


def _bf(a):
    return np.ascontiguousarray(a.astype(ml_dtypes.bfloat16))


def build(debug=()):
    nc = bass.Bass("TRN2", target_bir_lowering=False, debug=False)
    p = Prog(nc)

    def din(name, shape, dt=F32):
        return nc.dram_tensor(name, shape, dt, kind="ExternalInput")

    def dint(name, shape, dt=F32):
        return nc.dram_tensor(name, shape, dt)

    xT = din("xT", [D, T])
    wqT = din("wqT", [D, D]); wkT = din("wkT", [D, D])
    wvT = din("wvT", [D, D]); woT = din("woT", [D, D])
    cosT = din("cosT", [128, T]); sinT = din("sinT", [128, T])
    signv = din("signv", [128, 1]); qnv = din("qnv", [128, 1]); knv = din("knv", [128, 1])
    e64 = din("e64", [128, 2], BF16)
    bc2 = din("bc2", [2, 128])
    swapT = din("swapT", [128, 128], BF16)
    ones32 = din("ones32", [128, 32], BF16)
    g128 = din("g128", [128, 128])
    ones1f = din("ones1f", [1, 128])
    ones128f = din("ones128f", [128, 1])
    ones128b = din("ones128b", [128, 1], BF16)
    clipx = din("clipx", [1, 1]); clipd = din("clipd", [1, 1])
    wgTd = din("wgT", [D, I]); wuTd = din("wuT", [D, I]); wdTd = din("wdT", [I, D])
    wgd = din("wg", [I, D]); wud = din("wu", [I, D]); wdd = din("wd", [D, I])

    outT = nc.dram_tensor("outT", [D, T], F32, kind="ExternalOutput")

    ag_in = dint("ag_in", [2, D, T], BF16)
    ag_out = dint("ag_out", [4, 2, D, T], BF16)
    ar1_in = dint("ar1_in", [1, 2]); ar1_out = dint("ar1_out", [1, 2])
    ar2_in = dint("ar2_in", [1, 1]); ar2_out = dint("ar2_out", [1, 1])
    ar3_in = dint("ar3_in", [1, 1]); ar3_out = dint("ar3_out", [1, 1])
    wgi_d = dint("wgi_d", [D, I], BF16)
    wui_d = dint("wui_d", [D, I], BF16)
    wdi_d = dint("wdi_d", [I, D], BF16)
    y_d = dint("y_d", [D, T])
    h_d = dint("h_d", [I, T], F32)

    dbg = {}
    shapes = {"dbg_attn": ([D, T], BF16), "dbg_y": ([D, T], F32),
              "dbg_xq": ([D, T], BF16), "dbg_m": ([1, 2], F32),
              "dbg_h": ([I, T], F32), "dbg_wgint": ([128, NCH * I], BF16),
              "dbg_scales": ([128, 64], F32)}
    for name in debug:
        sh, dt = shapes[name]
        dbg[name] = nc.dram_tensor(name, sh, dt, kind="ExternalOutput")

    ctx = p.ctx
    sb = lambda name, shape, dt=F32: ctx.enter_context(nc.sbuf_tensor(name, shape, dt))

    PA = ctx.enter_context(nc.psum_tensor("PA", [128, 2048], F32))
    PB = ctx.enter_context(nc.psum_tensor("PB", [128, 2048], F32))

    signv_s = sb("signv_s", [128, 1])
    qnv_s = sb("qnv_s", [128, 1]); qnv8_s = sb("qnv8_s", [128, 1]); knv_s = sb("knv_s", [128, 1])
    e64_s = sb("e64_s", [128, 2], BF16)
    bc2_s = sb("bc2_s", [2, 128])
    swap_s = sb("swap_s", [128, 128], BF16)
    ones32_s = sb("ones32_s", [128, 32], BF16)
    g128_s = sb("g128_s", [128, 128])
    ones1f_s = sb("ones1f_s", [1, 128])
    ones128f_s = sb("ones128f_s", [128, 1])
    ones128b_s = sb("ones128b_s", [128, 1], BF16)
    clip_s = sb("clip_s", [1, 2])
    bias_s = sb("bias_s", [128, 1])
    norm2_s = sb("norm2_s", [2, 32])
    small_s = sb("small_s", [1, 64])
    small2_s = sb("small2_s", [128, 8])
    sgvec_s = sb("sgvec_s", [128, ICH])
    suvec_s = sb("suvec_s", [128, ICH])
    sdvec_s = sb("sdvec_s", [128, NCH])
    wscg_s = sb("wscg_s", [128, ICH])
    wscu_s = sb("wscu_s", [128, ICH])
    wscd_s = sb("wscd_s", [128, NCH])
    rscg_s = sb("rscg_s", [128, ICH])
    rscu_s = sb("rscu_s", [128, ICH])
    rscd_s = sb("rscd_s", [128, NCH])
    hm88_s = sb("hm88_s", [128, 96])
    rx_s = sb("rx_s", [128, 1])
    rd_s = sb("rd_s", [128, 1])
    sx_s = sb("sx_s", [128, 1])

    A = Arena(nc, ARENA_KB)
    qhat = A.t("qhat", [128, NCH * T], BF16, 0)
    attn_acc = A.t("attn_acc", [128, NCH * T], BF16, 32)
    mrow_s = A.t("mrow_s", [1, T], F32, 64)
    rrow_s = A.t("rrow_s", [1, T], F32, 72)
    vrow_s = A.t("vrow_s", [1, T], F32, 80)
    trow_s = A.t("trow_s", [1, T], F32, 88)
    y_s = A.t("y_s", [128, NCH * T], F32, 96)
    xq_s = A.t("xq_s", [128, NCH * T], BF16, 160)
    x_bf = A.t("x_bf", [128, NCH * T], BF16, 32)
    w_bf = A.t("w_bf", [128, NCH * D], BF16, 64)
    stage = [A.t(f"stage{i}", [128, T], F32, 80 + 8 * i) for i in range(2)]
    kf_c = [A.t(f"kf{i}", [128, T], F32, 96 + 8 * i) for i in range(2)]
    kpre = [A.t(f"kpre{i}", [128, T], BF16, 112 + 4 * i) for i in range(2)]
    sqb = [A.t(f"sqb{i}", [128, T], BF16, 120 + 4 * i) for i in range(2)]
    tmpf = [A.t("tmpf0", [128, T], F32, 128)] * 2
    khc = [A.t(f"khc{i}", [128, T], BF16, 136 + 4 * i) for i in range(2)]
    vtile = [A.t(f"vtile{i}", [128, 512], BF16, 144 + i) for i in range(4)]
    arow = A.t("arow", [2, T], F32, 148)
    arow2 = A.t("arow2", [2, T], F32, 156)
    cos_s = A.t("cos_s", [128, T], F32, 164)
    sins_s = A.t("sins_s", [128, T], F32, 172)
    sinr_s = A.t("sinr_s", [128, T], F32, 180)
    khp = [A.t(f"khp{i}", [128, S], BF16, 64 + 16 * i) for i in range(2)]
    vhp = [A.t(f"vhp{i}", [128, S], BF16, 96 + 16 * i) for i in range(2)]
    Pslot = [A.t(f"Pslot{i}", [128, 1024], BF16, 128 + 2 * i) for i in range(3)]
    sumsb = A.t("sumsb", [128, 512], F32, 134)
    recb = A.t("recb", [128, 512], F32, 136)
    rbw = A.t("rbw", [128, I], F32, 138)
    wrow = [A.t(f"wrow{i}", [128, I], F32, 149 + 11 * i) for i in range(2)]
    wqst_b = [A.t(f"wqst{i}", [128, I], F32, 171 + 11 * i) for i in range(2)]
    wint = A.t("wint", [128, I], BF16, 193)
    wo_bf = A.t("wo_bf", [128, NCH * D], BF16, 0)
    cstage = [A.t(f"cstage{i}", [128, 1024], F32, 16 + 4 * i) for i in range(2)]
    xres = [A.t(f"xres{i}", [128, 512], F32, 24 + 2 * i) for i in range(2)]
    sqst = [A.t(f"sqst{i}", [128, 512], BF16, 28 + i) for i in range(2)]
    mb_s = A.t("mb_s", [128, T], F32, 0)
    rb_s = A.t("rb_s", [128, T], F32, 8)
    xtmp = A.t("xtmp", [128, T], F32, 16)
    xm8_s = A.t("xm8_s", [128, 8], F32, 24)
    wu_int = A.t("wu_int", [128, NCH * I], BF16, 0)
    silg = [A.t(f"silg{i}", [128, 512], F32, 44 + 2 * i) for i in range(2)]
    hpre = [A.t(f"hpre{i}", [128, 512], F32, 48 + 2 * i) for i in range(2)]
    wg_int = A.t("wg_int", [128, NCH * I], BF16, 96)
    wdstage = [A.t(f"wdstage{i}", [128, 1024], BF16, 140 + 2 * i) for i in range(2)]
    y_ld = A.t("y_ld", [128, NCH * 512], F32, 96)
    z_s = A.t("z_s", [128, NCH * 512], F32, 112)
    sqz = [A.t(f"sqz{i}", [128, 512], BF16, 128 + i) for i in range(2)]
    mbq_s = A.t("mbq_s", [128, 512], F32, 130)
    rqb_s = A.t("rqb_s", [128, 512], F32, 132)
    ttile = A.t("ttile", [128, 512], F32, 144)
    hstage = [A.t(f"hstage{i}", [128, 512], F32, 146 + 2 * i) for i in range(2)]
    hq_t = [A.t(f"hq{i}", [128, 512], BF16, 151 + i) for i in range(2)]
    outst = [A.t(f"outst{i}", [128, 512], F32, 153 + 2 * i) for i in range(2)]

    def ld(e, dst, src):
        return e.dma_start(out=dst, in_=src)

    consts = [(cos_s[:], cosT.ap()), (sinr_s[:], sinT.ap()), (signv_s[:], signv.ap()),
              (qnv_s[:], qnv.ap()), (knv_s[:], knv.ap()), (e64_s[:], e64.ap()),
              (bc2_s[:], bc2.ap()), (swap_s[:], swapT.ap()), (ones32_s[:], ones32.ap()),
              (g128_s[:], g128.ap()), (ones1f_s[:], ones1f.ap()), (ones128f_s[:], ones128f.ap()),
              (ones128b_s[:], ones128b.ap()),
              (small_s[0:1, 0:1], clipx.ap()), (small_s[0:1, 1:2], clipd.ap())]
    for dst, src in consts:
        p.emit("sync", lambda e, d=dst, s=src: ld(e, d, s), inc="c_ld", inc_n=16)
    C_CONST = p.n("c_ld")

    xTv = xT.ap().rearrange("(c p) t -> c p t", p=128)
    for c in range(NCH):
        p.emit("sync", lambda e, c=c: ld(e, stage[c % 2][:], xTv[c]),
               wait=[("c_xcast", c - 1)], inc="c_xld", inc_n=16)
        p.emit("vector", lambda e, c=c: e.tensor_copy(x_bf[:, c * T:(c + 1) * T], stage[c % 2][:]),
               wait=[("c_xld", 16 * (c + 1))], inc="c_xcast")
    C_XBF = p.n("c_xcast")

    def load_weight_bf(dram_t, dst_bf, tag, extra=()):
        wv_ = dram_t.ap().rearrange("(c p) f -> c p f", p=128)
        for c in range(NCH):
            w = [("c_xcast", C_XBF), (tag + "cast", c - 1)] + list(extra)
            p.emit("sync", lambda e, c=c, wv_=wv_: ld(e, stage[c % 2][:, 0:D], wv_[c]),
                   wait=w, inc=tag + "ld", inc_n=16)
            p.emit("vector", lambda e, c=c, dst_bf=dst_bf: e.tensor_copy(
                dst_bf[:, c * D:(c + 1) * D], stage[c % 2][:, 0:D]),
                   wait=[(tag + "ld", 16 * (c + 1))], inc=tag + "cast")
        return p.n(tag + "cast")

    C_WK = load_weight_bf(wkT, w_bf, "wk")

    p.emit("vector", lambda e: e.memset(small_s[0:1, 32:48], 0.0),
           wait=[("c_ld", C_CONST)], inc="c_prep0")
    p.emit("vector", lambda e: e.tensor_scalar(sins_s[:], sinr_s[:], signv_s[:], None, OP.mult),
           wait=[("c_ld", C_CONST)], inc="c_prep0")
    p.emit("vector", lambda e: e.tensor_scalar(qnv8_s[:], qnv_s[:], 0.125, None, OP.mult),
           wait=[("c_ld", C_CONST)], inc="c_prep0")
    p.emit("scalar", lambda e: e.activation(clip_s[0:1, 0:2], small_s[0:1, 0:2], ACT.Exp,
                                            bias=small_s[0:1, 32:33], scale=-1.0),
           wait=[("c_prep0", 3)], inc="csg")
    p.emit("vector", lambda e: e.tensor_scalar(clip_s[0:1, 0:2], clip_s[0:1, 0:2], 1.0, None, OP.add),
           wait=[("csg", 1)], inc="csg")
    p.emit("vector", lambda e: e.reciprocal(clip_s[0:1, 0:2], clip_s[0:1, 0:2]),
           wait=[("csg", 2)], inc="csg")

    def qk_phase(tag, wbuf, wsem, w_cast_count, nvec, out_emit, out_wait, norm_col,
                 first_pa=(), first_pb=()):
        for fo in range(NCH):
            par = fo % 2
            for tt in range(4):
                for ki in range(NCH):
                    w = []
                    if tt == 0 and ki == 0:
                        w = [("c_xcast", C_XBF), (wsem, w_cast_count),
                             (tag + "evac", fo)] + (list(first_pa) if fo == 0 else [])
                    p.emit("tensor", lambda e, fo=fo, tt=tt, ki=ki:
                           e.matmul(PA[:, tt * 512:(tt + 1) * 512],
                                    wbuf[:, ki * D + fo * 128: ki * D + (fo + 1) * 128],
                                    x_bf[:, ki * T + tt * 512: ki * T + (tt + 1) * 512],
                                    start=(ki == 0), stop=(ki == NCH - 1)),
                           wait=w, inc=(tag + "proj") if ki == NCH - 1 else None)
            p.emit("vector", lambda e, par=par: e.tensor_copy(kf_c[par][:], PA[:]),
                   wait=[(tag + "proj", (fo + 1) * 4), (tag + "pre", fo - 1)],
                   inc=tag + "evac")
            p.emit("vector", lambda e, par=par: e.tensor_tensor(sqb[par][:], kf_c[par][:], kf_c[par][:], OP.mult),
                   wait=[(tag + "evac", fo + 1), (tag + "s2n", 3 * (fo - 1) + 2)], inc=tag + "sq")
            for n in range(4):
                w = [(tag + "sq", fo + 1), ("c_ld", C_CONST),
                     (tag + "s2n", 3 * fo), (tag + "r1", fo)]
                if fo == 0:
                    w += list(first_pb)
                p.emit("tensor", lambda e, par=par, n=n: e.matmul(
                    PB[0:2, n * 512:(n + 1) * 512], e64_s[:], sqb[par][:, n * 512:(n + 1) * 512],
                    start=True, stop=True),
                       wait=w, inc=(tag + "ssmm") if n == 3 else None)
            p.emit("vector", lambda e: e.tensor_scalar(arow[:], PB[0:2, :], 1.0 / 64, EPS_RMS, OP.mult, OP.add),
                   wait=[(tag + "ssmm", fo + 1)], inc=tag + "tal")
            p.emit("scalar", lambda e: e.activation(arow2[:], arow[:], ACT.Ln),
                   wait=[(tag + "tal", 3 * fo + 1)], inc=tag + "tal")
            p.emit("scalar", lambda e: e.activation(arow[:], arow2[:], ACT.Exp, scale=-0.5),
                   wait=[(tag + "tal", 3 * fo + 2)], inc=tag + "tal")
            for n in range(4):
                p.emit("tensor", lambda e, n=n: e.matmul(
                    PB[:, n * 512:(n + 1) * 512], bc2_s[:], arow[:, n * 512:(n + 1) * 512],
                    start=True, stop=True),
                       wait=[(tag + "tal", 3 * fo + 3)], inc=(tag + "abc") if n == 3 else None)
            p.emit("vector", lambda e, par=par, nvec=nvec: e.scalar_tensor_tensor(
                kpre[par][:], kf_c[par][:], nvec, PB[:], OP.mult, OP.mult),
                   wait=[(tag + "abc", fo + 1), ("c_prep0", 3)], inc=tag + "pre")
            for n in range(4):
                p.emit("tensor", lambda e, par=par, n=n: e.matmul(
                    PB[:, n * 512:(n + 1) * 512], swap_s[:], kpre[par][:, n * 512:(n + 1) * 512],
                    start=True, stop=True),
                       wait=[(tag + "pre", fo + 1)], inc=(tag + "swap") if n == 3 else None)
            p.emit("vector", lambda e, par=par: e.tensor_tensor(tmpf[par][:], PB[:], sins_s[:], OP.mult),
                   wait=[(tag + "swap", fo + 1), (tag + "rope", fo)], inc=tag + "r1")
            p.emit("vector", lambda e, par=par: e.tensor_tensor(khc[par][:], kpre[par][:], cos_s[:], OP.mult),
                   wait=[(tag + "pre", fo + 1), out_wait(fo - 2)], inc=tag + "r2")
            p.emit("vector", lambda e, par=par: e.tensor_tensor(khc[par][:], khc[par][:], tmpf[par][:], OP.add),
                   wait=[(tag + "r1", fo + 1), (tag + "r2", fo + 1)], inc=tag + "rope")
            p.emit("vector", lambda e, par=par: e.tensor_tensor(sqb[par][:], khc[par][:], khc[par][:], OP.mult),
                   wait=[(tag + "rope", fo + 1)], inc=tag + "s2n")
            for n in range(4):
                p.emit("tensor", lambda e, par=par, n=n: e.matmul(
                    PB[0:2, n * 512:(n + 1) * 512], e64_s[:], sqb[par][:, n * 512:(n + 1) * 512],
                    start=True, stop=True),
                       wait=[(tag + "s2n", 3 * fo + 1), (tag + "r1", fo + 1)],
                       inc=(tag + "s2n") if n == 3 else None)
            p.emit("vector", lambda e, fo=fo: e.reduce_max(
                norm2_s[0:2, norm_col * 8 + fo: norm_col * 8 + fo + 1], PB[0:2, :], axis=AX),
                   wait=[(tag + "s2n", 3 * fo + 2)], inc=tag + "s2n")
            out_emit(fo, par)

    agk = ag_in.ap()

    def k_out(fo, par):
        p.emit("sync", lambda e, fo=fo, par=par: ld(e, agk[0, fo * 128:(fo + 1) * 128, :], khc[par][:]),
               wait=[("krope", fo + 1)], inc="kout", inc_n=16)

    def k_out_wait(fo_prev):
        return ("kout", 16 * (fo_prev + 1))

    def q_out(fo, par):
        p.emit("vector", lambda e, fo=fo, par=par: e.tensor_copy(qhat[:, fo * T:(fo + 1) * T], khc[par][:]),
               wait=[("qrope", fo + 1)], inc="qout")

    def q_out_wait(fo_prev):
        return ("qout", fo_prev + 1)

    qk_phase("k", w_bf, "wkcast", C_WK, knv_s[:], k_out, k_out_wait, 1)
    C_WV = load_weight_bf(wvT, w_bf, "wv", extra=[("kproj", 32)])

    vt_i = 0
    for tt in range(16):
        for fh in range(2):
            sl = vt_i % 4
            for ki in range(NCH):
                w = []
                if ki == 0:
                    w = [("wvcast", C_WV), ("vcp", vt_i - 3)]
                    if vt_i == 0:
                        w.append(("kevac", NCH))
                p.emit("tensor", lambda e, tt=tt, fh=fh, ki=ki, sl=sl:
                       e.matmul(PA[:, sl * 512:(sl + 1) * 512],
                                x_bf[:, ki * T + tt * 128: ki * T + (tt + 1) * 128],
                                w_bf[:, ki * D + fh * 512: ki * D + (fh + 1) * 512],
                                start=(ki == 0), stop=(ki == NCH - 1)),
                       wait=w, inc="vproj" if ki == NCH - 1 else None)
            p.emit("vector", lambda e, sl=sl: e.tensor_copy(vtile[sl][:], PA[:, sl * 512:(sl + 1) * 512]),
                   wait=[("vproj", vt_i + 1), ("vout", 64 * (vt_i - 3))], inc="vcp")
            for hpi in range(4):
                hp = fh * 4 + hpi
                p.emit("sync", lambda e, hp=hp, tt=tt, hpi=hpi, sl=sl: e.dma_start(
                    out=agk[1, hp * 128 + tt * 8: hp * 128 + (tt + 1) * 8, :]
                        .rearrange("r c -> (r c)").rearrange("(t d) -> t d", d=128),
                    in_=vtile[sl][:, hpi * 128:(hpi + 1) * 128]),
                       wait=[("vcp", vt_i + 1)], inc="vout", inc_n=16)
            vt_i += 1

    p.emit("gpsimd", lambda e: e.collective_compute(
        "AllGather", OP.bypass,
        replica_groups=[[0, 1, 2, 3], [4, 5, 6, 7]],
        ins=[ag_in.ap().opt()], outs=[ag_out.ap().opt()]),
           wait=[("kout", 16 * NCH), ("vout", 16 * 128)], inc="ag")

    C_WQ = load_weight_bf(wqT, w_bf, "wq", extra=[("vproj", 32)])
    qk_phase("q", w_bf, "wqcast", C_WQ, qnv8_s[:], q_out, q_out_wait, 0,
             first_pa=[("vcp", 32)], first_pb=[("ks2n", 3 * NCH)])

    p.emit("vector", lambda e: e.reduce_max(norm2_s[0:2, 16:17], norm2_s[0:2, 0:8], axis=AX),
           wait=[("qs2n", 3 * NCH)], inc="b1s")
    p.emit("vector", lambda e: e.reduce_max(norm2_s[0:2, 17:18], norm2_s[0:2, 8:16], axis=AX),
           wait=[("ks2n", 3 * NCH), ("b1s", 1)], inc="b1s")
    p.emit("gpsimd", lambda e: e.tensor_reduce(small_s[0:1, 2:4], norm2_s[0:2, 16:18], AXC, OP.max),
           wait=[("b1s", 2)], inc="b1s")
    p.emit("gpsimd", lambda e: ld(e, ar1_in.ap(), small_s[0:1, 2:4]),
           wait=[("b1s", 3)], inc="b1s", inc_n=16)
    p.emit("gpsimd", lambda e: e.collective_compute(
        "AllReduce", OP.max, replica_groups=[list(range(8))],
        ins=[ar1_in.ap().opt()], outs=[ar1_out.ap().opt()]),
           wait=[("b1s", 19)], inc="b1s")
    p.emit("gpsimd", lambda e: ld(e, small_s[0:1, 4:6], ar1_out.ap()),
           wait=[("b1s", 20)], inc="b1s", inc_n=16)
    p.emit("vector", lambda e: e.tensor_scalar(small_s[0:1, 4:6], small_s[0:1, 4:6], 1e-38, None, OP.add),
           wait=[("b1s", 36)], inc="b1s")
    p.emit("scalar", lambda e: e.activation(small_s[0:1, 6:8], small_s[0:1, 4:6], ACT.Ln),
           wait=[("b1s", 37)], inc="b1s")
    p.emit("vector", lambda e: e.reduce_sum(small_s[0:1, 8:9], small_s[0:1, 6:8], axis=AX),
           wait=[("b1s", 38)], inc="b1s")
    p.emit("scalar", lambda e: e.activation(small_s[0:1, 9:10], small_s[0:1, 8:9], ACT.Exp, scale=0.5),
           wait=[("b1s", 39)], inc="b1s")
    p.emit("vector", lambda e: e.tensor_scalar(small_s[0:1, 10:11], small_s[0:1, 9:10], -1.0, None, OP.mult),
           wait=[("b1s", 40)], inc="b1s")
    p.emit("tensor", lambda e: e.matmul(PB[:, 0:1], ones1f_s[:], small_s[0:1, 10:11], start=True, stop=True),
           wait=[("b1s", 41), ("qs2n", 3 * NCH)], inc="b1s")
    p.emit("vector", lambda e: e.tensor_copy(bias_s[:], PB[:, 0:1]),
           wait=[("b1s", 42)], inc="b1s")
    B1DONE = 43

    if "dbg_m" in dbg:
        p.emit("sync", lambda e: ld(e, dbg["dbg_m"].ap(), small_s[0:1, 4:6]),
               wait=[("b1s", 41)], inc="dbgs", inc_n=16)

    wq_steps = []
    wqc = {}

    def wqn(name):
        return wqc.get(name, 0)

    def wqstep(eng, fn, wait=(), inc=None, inc_n=1):
        wq_steps.append((eng, fn, [(s, v) for (s, v) in wait if v > 0], inc, inc_n))
        if inc:
            wqc[inc] = wqc.get(inc, 0) + inc_n

    def wq_weight(w_rm, wT_d, wi_d, nch_rows, nch_cols, wsc, rsc):
        rows_v = w_rm.ap().rearrange("(c p) d -> c p d", p=128)
        ncols_row = w_rm.shape[1]
        L0 = [wqn("wqldE"), wqn("wqldO")]
        R0 = wqn("wqred"); C0 = wqn("wqrsb")
        T0 = [wqn("wqtlE"), wqn("wqtlO")]
        Q0 = wqn("wqq"); S0_ = wqn("wqst")
        for rc in range(nch_rows):
            lsem = "wqldE" if rc % 2 == 0 else "wqldO"
            wqstep("gpsimd", lambda e, rc=rc: ld(e, wrow[rc % 2][:, 0:ncols_row], rows_v[rc]),
                   [("qs2n", 3 * NCH), ("wqred", R0 + rc - 1)], lsem, 16)
            wqstep("vector", lambda e, rc=rc: e.reduce_max(
                wsc[:, rc:rc + 1], wrow[rc % 2][:, 0:ncols_row],
                axis=AX, apply_absolute_value=True),
                   [(lsem, L0[rc % 2] + 16 * (rc // 2 + 1))], "wqred", 1)
        wqstep("vector", lambda e: e.reciprocal(rsc[:, 0:nch_rows], wsc[:, 0:nch_rows]),
               [("wqred", R0 + nch_rows), ("wqrsb", C0)], "wqrsb", 1)
        wqstep("vector", lambda e: e.tensor_scalar(rsc[:, 0:nch_rows], rsc[:, 0:nch_rows],
                                                   127.0, None, OP.mult),
               [("wqrsb", C0 + 1)], "wqrsb", 1)
        NI_ = nch_rows * 128
        for rc in range(nch_rows):
            wqstep("gpsimd", lambda e, rc=rc: e.dma_start(
                out=rbw[0:1, rc * 128:(rc + 1) * 128],
                in_=rsc[:, rc:rc + 1], allow_slow_non_contiguous=True),
                   [("wqrsb", C0 + 2), ("wqq", Q0)], "wqrsb", 16)
        for pp in range(1, 128):
            wqstep("gpsimd", lambda e, pp=pp, NI_=NI_: e.dma_start(
                out=rbw[pp:pp + 1, 0:NI_], in_=rbw[0:1, 0:NI_]),
                   [("wqrsb", C0 + 2 + 16 * nch_rows)], "wqrsb", 16)
        CB = C0 + 2 + 16 * nch_rows + 127 * 16
        wTv = wT_d.ap().rearrange("(c p) f -> c p f", p=128)
        wiv = wi_d.ap().rearrange("(c p) f -> c p f", p=128)
        for cc in range(nch_cols):
            tsem = "wqtlE" if cc % 2 == 0 else "wqtlO"
            wqstep("gpsimd", lambda e, cc=cc, NI_=NI_: ld(e, wqst_b[cc % 2][:, 0:NI_], wTv[cc]),
                   [("wqrsb", CB), ("wqq", Q0 + 3 * (cc - 1))], tsem, 16)
            wqstep("vector", lambda e, cc=cc, NI_=NI_: e.tensor_tensor(
                wqst_b[cc % 2][:, 0:NI_], wqst_b[cc % 2][:, 0:NI_], rbw[:, 0:NI_], OP.mult),
                   [(tsem, T0[cc % 2] + 16 * (cc // 2 + 1)), ("wqq", Q0 + 3 * cc)], "wqq", 1)
            wqstep("vector", lambda e, cc=cc, NI_=NI_: e.tensor_scalar(
                wqst_b[cc % 2][:, 0:NI_], wqst_b[cc % 2][:, 0:NI_], MAGIC, -MAGIC, OP.add, OP.add),
                   [("wqq", Q0 + 3 * cc + 1)], "wqq", 1)
            wqstep("vector", lambda e, cc=cc, NI_=NI_: e.tensor_scalar(
                wint[:, 0:NI_], wqst_b[cc % 2][:, 0:NI_], 127.0, -128.0, OP.min, OP.max),
                   [("wqq", Q0 + 3 * cc + 2), ("wqst", S0_ + 16 * cc)], "wqq", 1)
            wqstep("gpsimd", lambda e, cc=cc, NI_=NI_: e.dma_start(out=wiv[cc], in_=wint[:, 0:NI_]),
                   [("wqq", Q0 + 3 * cc + 3)], "wqst", 16)

    wq_weight(wgd, wgTd, wgi_d, ICH, NCH, wscg_s, rscg_s)
    wq_weight(wud, wuTd, wui_d, ICH, NCH, wscu_s, rscu_s)
    wq_weight(wdd, wdTd, wdi_d, NCH, ICH, wscd_s, rscd_s)
    N_WQST = wqn("wqst")
    WQ_DONE = [("wqst", N_WQST)]
    wq_pos = [0]

    def emit_wq_slice(frac):
        tgt = int(len(wq_steps) * frac)
        while wq_pos[0] < tgt:
            eng, fn, w, incn, incs = wq_steps[wq_pos[0]]
            p.emit(eng, fn, wait=w, inc=incn, inc_n=incs)
            wq_pos[0] += 1

    slot_ps = [PA[:, 0:1024], PA[:, 1024:2048], PB[:, 0:1024]]
    av_ps = PB[:, 1024:1536]
    sums_ps = PB[:, 1536:2048]
    agv = ag_out.ap()

    for hp in range(HP):
        par = hp % 2
        for src in range(4):
            w = []
            if src == 0:
                w = [("ag", 1), ("qs2n", 3 * NCH), ("qout", NCH)]
                if hp >= 2:
                    w.append(("qte", 4 * 4 * (hp - 1)))
            p.emit("sync", lambda e, hp=hp, src=src, par=par: e.dma_start(
                out=khp[par][:, src * T:(src + 1) * T],
                in_=agv[src, 0, hp * 128:(hp + 1) * 128, :]),
                   wait=w, inc="kvld", inc_n=16)
            p.emit("sync", lambda e, hp=hp, src=src, par=par: e.dma_start(
                out=vhp[par][:, src * T:(src + 1) * T],
                in_=agv[src, 1, hp * 128:(hp + 1) * 128, :]
                    .rearrange("r c -> (r c)").rearrange("(lc p d) -> p lc d", p=128, d=128)),
                   inc="kvld", inc_n=16)
        for qt in range(QT):
            qtg = hp * QT + qt
            E0 = qtg * KCH
            S0 = qtg * 32
            q_lo = hp * T + qt * 512
            for c in range(KCH):
                g = E0 + c
                sl = c % 3
                w = [("exp", E0 if c == 0 else g - 2)]
                if c == 0 and qt == 0:
                    w.append(("kvld", 128 * (hp + 1)))
                p.emit("tensor", lambda e, par=par, c=c, q_lo=q_lo, sl=sl: e.matmul(
                    slot_ps[sl][:, 0:512], khp[par][0:64, c * 128:(c + 1) * 128],
                    qhat[0:64, q_lo:q_lo + 512], start=True, stop=True, tile_position=(0, 0)),
                       wait=w)
                p.emit("tensor", lambda e, par=par, c=c, q_lo=q_lo, sl=sl: e.matmul(
                    slot_ps[sl][:, 512:1024], khp[par][64:128, c * 128:(c + 1) * 128],
                    qhat[64:128, q_lo:q_lo + 512], start=True, stop=True, tile_position=(64, 0)),
                       inc="sc")
                if c >= 1:
                    cc = c - 1
                    ssl = cc % 3
                    w = [("exp", E0 + cc + 1)]
                    if cc == 0:
                        w.append(("qte", 4 * qtg))
                    p.emit("tensor", lambda e, par=par, cc=cc, ssl=ssl: e.matmul(
                        av_ps[0:64, :], vhp[par][:, cc * 128: cc * 128 + 64],
                        Pslot[ssl][:, 0:512], start=(cc == 0), stop=False,
                        tile_position=(0, 0)),
                           wait=w)
                    p.emit("tensor", lambda e, par=par, cc=cc, ssl=ssl: e.matmul(
                        av_ps[64:128, :], vhp[par][:, cc * 128 + 64: (cc + 1) * 128],
                        Pslot[ssl][:, 512:1024], start=(cc == 0), stop=False,
                        tile_position=(0, 64)),
                           inc="av")
                if c >= 2 and c % 2 == 0:
                    w = [("exp", E0 + c)]
                    if c == 2:
                        w.append(("qte", 4 * (qtg - 1) + 3))
                    for t4 in range(4):
                        cj = c - 2 + t4 // 2
                        half = t4 % 2
                        p.emit("tensor", lambda e, cj=cj, half=half, t4=t4: e.matmul(
                            sums_ps[t4 * 32:(t4 + 1) * 32, :], ones32_s[:],
                            Pslot[cj % 3][:, half * 512:(half + 1) * 512],
                            start=(cj < 2), stop=False, tile_position=(0, t4 * 32)),
                               wait=w if t4 == 0 else (), inc="sums" if t4 == 3 else None)
                w = [("sc", g + 1)]
                if g == 0:
                    w.append(("b1s", B1DONE))
                if c >= 3:
                    w += [("av", g - 2), ("sums", S0 + (c - 3) // 2 + 1)]
                p.emit("scalar", lambda e, sl=sl: e.activation(
                    Pslot[sl][:], slot_ps[sl], ACT.Exp, bias=bias_s[:], scale=1.0),
                       wait=w, inc="exp")
            cc = KCH - 1
            p.emit("tensor", lambda e, par=par, cc=cc: e.matmul(
                av_ps[0:64, :], vhp[par][:, cc * 128: cc * 128 + 64],
                Pslot[cc % 3][:, 0:512], start=False, stop=True, tile_position=(0, 0)),
                   wait=[("exp", E0 + KCH)])
            p.emit("tensor", lambda e, par=par, cc=cc: e.matmul(
                av_ps[64:128, :], vhp[par][:, cc * 128 + 64: (cc + 1) * 128],
                Pslot[cc % 3][:, 512:1024], start=False, stop=True, tile_position=(0, 64)),
                   inc="av")
            for t4 in range(4):
                cj = KCH - 2 + t4 // 2
                half = t4 % 2
                p.emit("tensor", lambda e, cj=cj, half=half, t4=t4: e.matmul(
                    sums_ps[t4 * 32:(t4 + 1) * 32, :], ones32_s[:],
                    Pslot[cj % 3][:, half * 512:(half + 1) * 512],
                    start=False, stop=True, tile_position=(0, t4 * 32)),
                       wait=[("exp", E0 + KCH)] if t4 == 0 else (),
                       inc="sums" if t4 == 3 else None)
            p.emit("vector", lambda e: e.tensor_copy(sumsb[:], sums_ps),
                   wait=[("sums", S0 + 32), ("qte", 4 * qtg)], inc="qte")
            p.emit("tensor", lambda e: e.matmul(PB[:, 1536:2048], g128_s[:], sumsb[:],
                                                start=True, stop=True),
                   wait=[("qte", 4 * qtg + 1), ("exp", E0 + KCH)], inc="qte")
            p.emit("vector", lambda e: e.reciprocal(recb[:], PB[:, 1536:2048]),
                   wait=[("qte", 4 * qtg + 2)], inc="qte")
            p.emit("vector", lambda e, q_lo=q_lo: e.tensor_tensor(
                attn_acc[:, q_lo:q_lo + 512], av_ps, recb[:], OP.mult),
                   wait=[("qte", 4 * qtg + 3), ("av", (qtg + 1) * KCH)], inc="qte")
        emit_wq_slice((hp + 1) / HP)

    N_QTE = 4 * HP * QT

    if "dbg_attn" in dbg:
        da_ = dbg["dbg_attn"].ap().rearrange("(c p) t -> c p t", p=128)
        for c in range(NCH):
            p.emit("sync", lambda e, c=c: ld(e, da_[c], attn_acc[:, c * T:(c + 1) * T]),
                   wait=[("qte", N_QTE)], inc="dbgs", inc_n=16)

    woTv = woT.ap().rearrange("(c p) f -> c p f", p=128)
    for c in range(NCH):
        w = [("wocast", c - 1)]
        if c == 0:
            w += [("qte", N_QTE)] + WQ_DONE
        p.emit("sync", lambda e, c=c: ld(e, cstage[c % 2][:, 0:D], woTv[c]),
               wait=w, inc="wold", inc_n=16)
        p.emit("vector", lambda e, c=c: e.tensor_copy(wo_bf[:, c * D:(c + 1) * D],
                                                      cstage[c % 2][:, 0:D]),
               wait=[("wold", 16 * (c + 1))], inc="wocast")
    C_WO = p.n("wocast")

    xTv2 = xT.ap().rearrange("(c p) t -> c p t", p=128)
    wo_i = 0
    for qt in range(QT):
        for oc in range(NCH):
            sl = wo_i % 4
            for ic in range(NCH):
                w = []
                if ic == 0:
                    w = [("qte", N_QTE), ("wocast", C_WO), ("yev", wo_i - 3)]
                p.emit("tensor", lambda e, oc=oc, qt=qt, ic=ic, sl=sl: e.matmul(
                    PA[:, sl * 512:(sl + 1) * 512],
                    wo_bf[:, ic * D + oc * 128:(ic * D) + (oc + 1) * 128],
                    attn_acc[:, ic * T + qt * 512: ic * T + qt * 512 + 512],
                    start=(ic == 0), stop=(ic == NCH - 1)),
                       wait=w, inc="wop" if ic == NCH - 1 else None)
            p.emit("sync", lambda e, oc=oc, qt=qt, i=wo_i: ld(
                e, xres[i % 2][:], xTv2[oc][:, qt * 512:(qt + 1) * 512]),
                   wait=[("yev", wo_i - 1)], inc="xld", inc_n=16)
            p.emit("vector", lambda e, oc=oc, qt=qt, sl=sl, i=wo_i: e.tensor_tensor(
                y_s[:, oc * T + qt * 512: oc * T + qt * 512 + 512],
                PA[:, sl * 512:(sl + 1) * 512], xres[i % 2][:], OP.add),
                   wait=[("wop", wo_i + 1), ("xld", 16 * (wo_i + 1))], inc="yev")
            p.emit("vector", lambda e, oc=oc, qt=qt, i=wo_i: e.tensor_tensor(
                sqst[i % 2][:], y_s[:, oc * T + qt * 512: oc * T + qt * 512 + 512],
                y_s[:, oc * T + qt * 512: oc * T + qt * 512 + 512], OP.mult),
                   wait=[("yev", wo_i + 1)], inc="sqev")
            w = [("yev", wo_i + 1)]
            if oc == 0:
                w.append(("strs", 7 * (qt - 1) + 1))
            p.emit("tensor", lambda e, oc=oc, qt=qt: e.matmul(
                PB[0:1, 0:512], ones128f_s[:],
                y_s[:, oc * T + qt * 512: oc * T + qt * 512 + 512],
                start=(oc == 0), stop=(oc == NCH - 1)),
                   wait=w, inc="sty" if oc == NCH - 1 else None)
            wq_ = [("sqev", wo_i + 1)]
            if oc == 0:
                wq_.append(("strs", 7 * (qt - 1) + 2))
            p.emit("tensor", lambda e, oc=oc, qt=qt, i=wo_i: e.matmul(
                PB[0:1, 512:1024], ones128b_s[:], sqst[i % 2][:],
                start=(oc == 0), stop=(oc == NCH - 1)),
                   wait=wq_, inc="stq" if oc == NCH - 1 else None)
            wo_i += 1
        qsl = (qt * 512, (qt + 1) * 512)
        p.emit("vector", lambda e, qsl=qsl: e.tensor_scalar(
            mrow_s[0:1, qsl[0]:qsl[1]], PB[0:1, 0:512], 1.0 / D, None, OP.mult),
               wait=[("sty", qt + 1), ("strs", 7 * qt)], inc="strs")
        p.emit("vector", lambda e, qsl=qsl: e.tensor_scalar(
            vrow_s[0:1, qsl[0]:qsl[1]], PB[0:1, 512:1024], 1.0 / D, None, OP.mult),
               wait=[("stq", qt + 1), ("strs", 7 * qt + 1)], inc="strs")
        p.emit("vector", lambda e, qsl=qsl: e.tensor_tensor(
            trow_s[0:1, qsl[0]:qsl[1]], mrow_s[0:1, qsl[0]:qsl[1]],
            mrow_s[0:1, qsl[0]:qsl[1]], OP.mult),
               wait=[("strs", 7 * qt + 2)], inc="strs")
        p.emit("vector", lambda e, qsl=qsl: e.tensor_tensor(
            vrow_s[0:1, qsl[0]:qsl[1]], vrow_s[0:1, qsl[0]:qsl[1]],
            trow_s[0:1, qsl[0]:qsl[1]], OP.subtract),
               wait=[("strs", 7 * qt + 3)], inc="strs")
        p.emit("vector", lambda e, qsl=qsl: e.tensor_scalar(
            vrow_s[0:1, qsl[0]:qsl[1]], vrow_s[0:1, qsl[0]:qsl[1]], EPS_LN, None, OP.add),
               wait=[("strs", 7 * qt + 4)], inc="strs")
        p.emit("scalar", lambda e, qsl=qsl: e.activation(
            trow_s[0:1, qsl[0]:qsl[1]], vrow_s[0:1, qsl[0]:qsl[1]], ACT.Ln),
               wait=[("strs", 7 * qt + 5)], inc="strs")
        p.emit("scalar", lambda e, qsl=qsl: e.activation(
            rrow_s[0:1, qsl[0]:qsl[1]], trow_s[0:1, qsl[0]:qsl[1]], ACT.Exp, scale=-0.5),
               wait=[("strs", 7 * qt + 6)], inc="strs")
    N_STRS = 7 * QT

    for qt in range(QT):
        p.emit("tensor", lambda e, qt=qt: e.matmul(
            PB[:, 512:1024], ones1f_s[:], mrow_s[0:1, qt * 512:(qt + 1) * 512],
            start=True, stop=True),
               wait=[("strs", N_STRS), ("mbs", 4 * qt)], inc="mbs")
        p.emit("vector", lambda e, qt=qt: e.tensor_copy(
            mb_s[:, qt * 512:(qt + 1) * 512], PB[:, 512:1024]),
               wait=[("mbs", 4 * qt + 1)], inc="mbs")
        p.emit("tensor", lambda e, qt=qt: e.matmul(
            PB[:, 512:1024], ones1f_s[:], rrow_s[0:1, qt * 512:(qt + 1) * 512],
            start=True, stop=True),
               wait=[("mbs", 4 * qt + 2)], inc="mbs")
        p.emit("vector", lambda e, qt=qt: e.tensor_copy(
            rb_s[:, qt * 512:(qt + 1) * 512], PB[:, 512:1024]),
               wait=[("mbs", 4 * qt + 3)], inc="mbs")
    for oc in range(NCH):
        p.emit("vector", lambda e, oc=oc: e.tensor_tensor(
            xtmp[:], y_s[:, oc * T:(oc + 1) * T], mb_s[:], OP.subtract),
               wait=[("mbs", 4 * QT), ("yev", QT * NCH), ("xms", 3 * oc)], inc="xms")
        p.emit("vector", lambda e: e.tensor_tensor(xtmp[:], xtmp[:], rb_s[:], OP.mult),
               wait=[("xms", 3 * oc + 1)], inc="xms")
        p.emit("vector", lambda e, oc=oc: e.reduce_max(
            xm8_s[:, oc:oc + 1], xtmp[:], axis=AX, apply_absolute_value=True),
               wait=[("xms", 3 * oc + 2)], inc="xms")
    p.emit("vector", lambda e: e.reduce_max(small2_s[:, 0:1], xm8_s[:, 0:8], axis=AX),
           wait=[("xms", 3 * NCH)], inc="xms")
    XMS_D = 3 * NCH + 1
    p.emit("gpsimd", lambda e: e.tensor_reduce(small_s[0:1, 16:17], small2_s[:, 0:1], AXC, OP.max),
           wait=[("xms", XMS_D)], inc="xss")
    p.emit("gpsimd", lambda e: ld(e, ar2_in.ap(), small_s[0:1, 16:17]),
           wait=[("xss", 1)], inc="xss", inc_n=16)
    p.emit("gpsimd", lambda e: e.collective_compute(
        "AllReduce", OP.max, replica_groups=[list(range(8))],
        ins=[ar2_in.ap().opt()], outs=[ar2_out.ap().opt()]),
           wait=[("xss", 17)], inc="xss")
    p.emit("gpsimd", lambda e: ld(e, small_s[0:1, 17:18], ar2_out.ap()),
           wait=[("xss", 18)], inc="xss", inc_n=16)
    p.emit("vector", lambda e: e.tensor_tensor(small_s[0:1, 18:19], small_s[0:1, 17:18],
                                               clip_s[0:1, 0:1], OP.mult),
           wait=[("xss", 34), ("csg", 3)], inc="xss")
    p.emit("vector", lambda e: e.tensor_scalar(small_s[0:1, 18:19], small_s[0:1, 18:19],
                                               1.0 / 127.0, None, OP.mult),
           wait=[("xss", 35)], inc="xss")
    p.emit("vector", lambda e: e.reciprocal(small_s[0:1, 19:20], small_s[0:1, 18:19]),
           wait=[("xss", 36)], inc="xss")
    p.emit("tensor", lambda e: e.matmul(PB[:, 0:1], ones1f_s[:], small_s[0:1, 18:19],
                                        start=True, stop=True),
           wait=[("xss", 37), ("strs", N_STRS)], inc="xss")
    p.emit("vector", lambda e: e.tensor_copy(sx_s[:], PB[:, 0:1]),
           wait=[("xss", 38)], inc="xss")
    p.emit("tensor", lambda e: e.matmul(PB[:, 0:1], ones1f_s[:], small_s[0:1, 19:20],
                                        start=True, stop=True),
           wait=[("xss", 39)], inc="xss")
    p.emit("vector", lambda e: e.tensor_copy(rx_s[:], PB[:, 0:1]),
           wait=[("xss", 40)], inc="xss")
    p.emit("vector", lambda e: e.tensor_scalar(rb_s[:], rb_s[:], rx_s[:], None, OP.mult),
           wait=[("xss", 41), ("xms", XMS_D)], inc="xss")
    XSS_D = 42
    for oc in range(NCH):
        p.emit("vector", lambda e, oc=oc: e.tensor_tensor(
            xtmp[:], y_s[:, oc * T:(oc + 1) * T], mb_s[:], OP.subtract),
               wait=[("xss", XSS_D), ("xqs", 4 * oc)], inc="xqs")
        p.emit("vector", lambda e: e.tensor_tensor(xtmp[:], xtmp[:], rb_s[:], OP.mult),
               wait=[("xqs", 4 * oc + 1)], inc="xqs")
        p.emit("vector", lambda e: e.tensor_scalar(xtmp[:], xtmp[:], MAGIC, -MAGIC, OP.add, OP.add),
               wait=[("xqs", 4 * oc + 2)], inc="xqs")
        p.emit("vector", lambda e, oc=oc: e.tensor_scalar(
            xq_s[:, oc * T:(oc + 1) * T], xtmp[:], 127.0, -128.0, OP.min, OP.max),
               wait=[("xqs", 4 * oc + 3)], inc="xqs")
    N_XQS = 4 * NCH
    ydv = y_d.ap().rearrange("(c p) t -> c p t", p=128)
    for oc in range(NCH):
        p.emit("sync", lambda e, oc=oc: ld(e, ydv[oc], y_s[:, oc * T:(oc + 1) * T]),
               wait=[("xqs", N_XQS)], inc="yspill", inc_n=16)
    N_YSPILL = p.n("yspill")

    if "dbg_y" in dbg:
        dy_ = dbg["dbg_y"].ap().rearrange("(c p) t -> c p t", p=128)
        for c in range(NCH):
            p.emit("sync", lambda e, c=c: ld(e, dy_[c], y_s[:, c * T:(c + 1) * T]),
                   wait=[("yev", QT * NCH), ("dbgs", p.n("dbgs"))], inc="dbgs", inc_n=16)
    if "dbg_xq" in dbg:
        dxq_ = dbg["dbg_xq"].ap().rearrange("(c p) t -> c p t", p=128)
        for c in range(NCH):
            p.emit("sync", lambda e, c=c: ld(e, dxq_[c], xq_s[:, c * T:(c + 1) * T]),
                   wait=[("xqs", N_XQS), ("dbgs", p.n("dbgs"))], inc="dbgs", inc_n=16)

    p.emit("vector", lambda e: e.tensor_scalar(sgvec_s[:, 0:ICH], wscg_s[:, 0:ICH],
                                               sx_s[:], 1.0 / 127.0, OP.mult, OP.mult),
           wait=[("xss", 39)] + WQ_DONE, inc="sgv")
    p.emit("vector", lambda e: e.tensor_scalar(suvec_s[:, 0:ICH], wscu_s[:, 0:ICH],
                                               sx_s[:], 1.0 / 127.0, OP.mult, OP.mult),
           wait=[("sgv", 1)], inc="sgv")
    wgiv = wgi_d.ap().rearrange("(c p) f -> c p f", p=128)
    wuiv = wui_d.ap().rearrange("(c p) f -> c p f", p=128)
    for c in range(NCH):
        w = [("yspill", N_YSPILL), ("wqst", 16 * NCH)] if c == 0 else []
        p.emit("sync", lambda e, c=c: ld(e, wg_int[:, c * I:(c + 1) * I], wgiv[c]),
               wait=w, inc="wgild", inc_n=16)
        w2 = [("xqs", N_XQS), ("wqst", 16 * 2 * NCH)] if c == 0 else []
        p.emit("sync", lambda e, c=c: ld(e, wu_int[:, c * I:(c + 1) * I], wuiv[c]),
               wait=w2, inc="wgild", inc_n=16)
    C_WGI = p.n("wgild")

    hdv = h_d.ap()
    gu_i = 0
    for qt in range(QT):
        for ic in range(ICH):
            j = gu_i % 2
            gps = [PA[:, 0:512], PB[:, 0:512]][j]
            ups = [PA[:, 512:1024], PB[:, 512:1024]][j]
            for dk in range(NCH):
                w = []
                if dk == 0:
                    w = [("wgild", C_WGI), ("sih", 3 * (gu_i - 2) + 2)]
                    if gu_i <= 1:
                        w.append(("xss", XSS_D))
                p.emit("tensor", lambda e, qt=qt, ic=ic, dk=dk, gps=gps: e.matmul(
                    gps, wg_int[:, dk * I + ic * 128: dk * I + (ic + 1) * 128],
                    xq_s[:, dk * T + qt * 512: dk * T + (qt + 1) * 512],
                    start=(dk == 0), stop=(dk == NCH - 1)),
                       wait=w)
                p.emit("tensor", lambda e, qt=qt, ic=ic, dk=dk, ups=ups: e.matmul(
                    ups, wu_int[:, dk * I + ic * 128: dk * I + (ic + 1) * 128],
                    xq_s[:, dk * T + qt * 512: dk * T + (qt + 1) * 512],
                    start=(dk == 0), stop=(dk == NCH - 1)),
                       inc="gup" if dk == NCH - 1 else None)
            p.emit("scalar", lambda e, ic=ic, j=j, gps=gps: e.activation(
                silg[j][:], gps, ACT.Silu, scale=sgvec_s[:, ic:ic + 1]),
                   wait=[("gup", gu_i + 1), ("sgv", 2), ("sih", 3 * (gu_i - 2) + 2)], inc="sih")
            p.emit("vector", lambda e, j=j, ic=ic, ups=ups: e.scalar_tensor_tensor(
                hpre[j][:], ups, suvec_s[:, ic:ic + 1], silg[j][:], OP.mult, OP.mult),
                   wait=[("sih", 3 * gu_i + 1), ("hst", 16 * (gu_i - 1))], inc="sih")
            p.emit("vector", lambda e, j=j, gi=gu_i: e.reduce_max(
                hm88_s[:, gi:gi + 1] if gi < 88 else hm88_s[:, 88:89],
                hpre[j][:], axis=AX, apply_absolute_value=True),
                   wait=[("sih", 3 * gu_i + 2)], inc="sih")
            p.emit("sync", lambda e, qt=qt, ic=ic, j=j: e.dma_start(
                out=hdv[ic * 128:(ic + 1) * 128, qt * 512:(qt + 1) * 512], in_=hpre[j][:]),
                   wait=[("sih", 3 * gu_i + 3)], inc="hst", inc_n=16)
            gu_i += 1
    N_GU = gu_i

    p.emit("vector", lambda e: e.reduce_max(small2_s[:, 1:2], hm88_s[:, 0:88], axis=AX),
           wait=[("sih", 3 * N_GU)], inc="hss")
    p.emit("gpsimd", lambda e: e.tensor_reduce(small_s[0:1, 24:25], small2_s[:, 1:2], AXC, OP.max),
           wait=[("hss", 1)], inc="hss")
    p.emit("gpsimd", lambda e: ld(e, ar3_in.ap(), small_s[0:1, 24:25]),
           wait=[("hss", 2)], inc="hss", inc_n=16)
    p.emit("gpsimd", lambda e: e.collective_compute(
        "AllReduce", OP.max, replica_groups=[list(range(8))],
        ins=[ar3_in.ap().opt()], outs=[ar3_out.ap().opt()]),
           wait=[("hss", 18)], inc="hss")
    p.emit("gpsimd", lambda e: ld(e, small_s[0:1, 25:26], ar3_out.ap()),
           wait=[("hss", 19)], inc="hss", inc_n=16)
    p.emit("vector", lambda e: e.tensor_tensor(small_s[0:1, 26:27], small_s[0:1, 25:26],
                                               clip_s[0:1, 1:2], OP.mult),
           wait=[("hss", 35)], inc="hss")
    p.emit("vector", lambda e: e.tensor_scalar(small_s[0:1, 26:27], small_s[0:1, 26:27],
                                               1.0 / 127.0, None, OP.mult),
           wait=[("hss", 36)], inc="hss")
    p.emit("vector", lambda e: e.reciprocal(small_s[0:1, 27:28], small_s[0:1, 26:27]),
           wait=[("hss", 37)], inc="hss")
    p.emit("tensor", lambda e: e.matmul(PB[:, 0:1], ones1f_s[:], small_s[0:1, 26:27],
                                        start=True, stop=True),
           wait=[("hss", 38), ("gup", N_GU)], inc="hss")
    p.emit("vector", lambda e: e.tensor_copy(rd_s[:], PB[:, 0:1]),
           wait=[("hss", 39)], inc="hss")
    p.emit("vector", lambda e: e.tensor_scalar(sdvec_s[:, 0:NCH], wscd_s[:, 0:NCH],
                                               rd_s[:], 1.0 / 127.0, OP.mult, OP.mult),
           wait=[("hss", 40)], inc="hss")
    p.emit("tensor", lambda e: e.matmul(PB[:, 0:1], ones1f_s[:], small_s[0:1, 27:28],
                                        start=True, stop=True),
           wait=[("hss", 41)], inc="hss")
    p.emit("vector", lambda e: e.tensor_copy(rd_s[:], PB[:, 0:1]),
           wait=[("hss", 42)], inc="hss")
    HSS_D = 43

    if "dbg_wgint" in dbg:
        p.emit("sync", lambda e: ld(e, dbg["dbg_wgint"].ap(), wg_int[:]),
               wait=[("wgild", C_WGI), ("dbgs", p.n("dbgs"))], inc="dbgs", inc_n=16)
    if "dbg_scales" in dbg:
        def dump_scales(e):
            return ld(e, dbg["dbg_scales"].ap().rearrange("p (a b) -> p a b", a=2)[:, 0, 0:ICH],
                      wscg_s[:, 0:ICH])
        p.emit("sync", dump_scales, wait=WQ_DONE + [("dbgs", p.n("dbgs"))], inc="dbgs", inc_n=16)
        p.emit("sync", lambda e: e.dma_start(
            out=dbg["dbg_scales"].ap().rearrange("p (a b) -> p a b", a=2)[:, 1, 0:1],
            in_=sx_s[:], allow_slow_non_contiguous=True),
               wait=[("xss", 39), ("dbgs", p.n("dbgs"))], inc="dbgs", inc_n=16)
    if "dbg_h" in dbg:
        dh_ = dbg["dbg_h"].ap().rearrange("(c p) t -> c p t", p=128)
        for c in range(ICH):
            p.emit("sync", lambda e, c=c: ld(e, dh_[c], hdv[c * 128:(c + 1) * 128, :]),
                   wait=[("hst", 16 * N_GU), ("dbgs", p.n("dbgs"))], inc="dbgs", inc_n=16)

    wdiv = wdi_d.ap().rearrange("(c p) f -> c p f", p=128)
    outv = outT.ap().rearrange("(c p) t -> c p t", p=128)
    dn_i = 0
    for qt in range(QT):
        L = LNS_QT * qt
        for oc in range(NCH):
            w = [("lns", LNS_QT * (qt - 1) + 12 + 3 * NCH)] if qt > 0 else []
            if qt == 0 and oc == 0:
                w = [("yspill", N_YSPILL), ("gup", N_GU)]
            p.emit("sync", lambda e, oc=oc, qt=qt: ld(
                e, y_ld[:, oc * 512:(oc + 1) * 512], ydv[oc][:, qt * 512:(qt + 1) * 512]),
                   wait=w, inc="yld", inc_n=16)
        for hc in range(ICH):
            j = dn_i % 2
            w = [("hst", 16 * N_GU)] if dn_i == 0 else [("hqs", 4 * (dn_i - 1))]
            p.emit("sync", lambda e, qt=qt, hc=hc, j=j: ld(
                e, hstage[j][:], hdv[hc * 128:(hc + 1) * 128, qt * 512:(qt + 1) * 512]),
                   wait=w + ([("wqst", N_WQST)] if dn_i == 0 else []), inc="hld", inc_n=16)
            p.emit("sync", lambda e, hc=hc, j=j: ld(e, wdstage[j][:], wdiv[hc]),
                   wait=([("hss", HSS_D), ("wqst", N_WQST)] if dn_i == 0 else []) + [("dnp", dn_i - 1)],
                   inc="wdld", inc_n=16)
            p.emit("vector", lambda e, j=j: e.tensor_scalar(
                ttile[:], hstage[j][:], rd_s[:], None, OP.mult),
                   wait=[("hld", 16 * (dn_i + 1)), ("hss", HSS_D),
                         ("lns", (LNS_QT * (qt - 1) + 43) if qt > 0 else 0),
                         ("hqs", 4 * dn_i)], inc="hqs")
            p.emit("vector", lambda e: e.tensor_scalar(ttile[:], ttile[:], MAGIC, -MAGIC, OP.add, OP.add),
                   wait=[("hqs", 4 * dn_i + 1)], inc="hqs")
            p.emit("vector", lambda e, j=j: e.tensor_scalar(
                hq_t[j][:], ttile[:], 127.0, -128.0, OP.min, OP.max),
                   wait=[("hqs", 4 * dn_i + 2)], inc="hqs")
            for oc in range(NCH):
                ps = (PA if oc < 4 else PB)[:, (oc % 4) * 512:((oc % 4) + 1) * 512]
                w = [("hqs", 4 * dn_i + 3), ("wdld", 16 * (dn_i + 1))]
                if hc == 0 and oc == 0 and qt > 0:
                    w += [("lns", LNS_QT * (qt - 1) + 47)]
                p.emit("tensor", lambda e, oc=oc, j=j, ps=ps, hc=hc: e.matmul(
                    ps, wdstage[j][:, oc * 128:(oc + 1) * 128], hq_t[j][:],
                    start=(hc == 0), stop=(hc == ICH - 1)),
                       wait=w, inc="dnp" if oc == NCH - 1 else None)
            p.emit("vector", lambda e: None, wait=[("dnp", dn_i + 1)], inc="hqs")
            dn_i += 1
        for oc in range(NCH):
            ps = (PA if oc < 4 else PB)[:, (oc % 4) * 512:((oc % 4) + 1) * 512]
            p.emit("vector", lambda e, oc=oc, ps=ps: e.tensor_scalar(
                z_s[:, oc * 512:(oc + 1) * 512], ps, sdvec_s[:, oc:oc + 1], None, OP.mult),
                   wait=[("dnp", (qt + 1) * ICH), ("lns", L + oc)], inc="lns")
        p.emit("tensor", lambda e, qt=qt: e.matmul(
            PB[:, 1536:2048], ones1f_s[:], mrow_s[0:1, qt * 512:(qt + 1) * 512],
            start=True, stop=True),
               wait=[("lns", L + 8)], inc="lns")
        p.emit("vector", lambda e: e.tensor_copy(mbq_s[:], PB[:, 1536:2048]),
               wait=[("lns", L + 9)], inc="lns")
        p.emit("tensor", lambda e, qt=qt: e.matmul(
            PB[:, 1536:2048], ones1f_s[:], rrow_s[0:1, qt * 512:(qt + 1) * 512],
            start=True, stop=True),
               wait=[("lns", L + 10)], inc="lns")
        p.emit("vector", lambda e: e.tensor_copy(rqb_s[:], PB[:, 1536:2048]),
               wait=[("lns", L + 11)], inc="lns")
        for oc in range(NCH):
            p.emit("vector", lambda e, oc=oc: e.tensor_tensor(
                ttile[:], y_ld[:, oc * 512:(oc + 1) * 512], mbq_s[:], OP.subtract),
                   wait=[("lns", L + 12 + 3 * oc), ("yld", 16 * 8 * (qt + 1)),
                         ("hqs", 4 * (qt + 1) * ICH)], inc="lns")
            p.emit("vector", lambda e: e.tensor_tensor(ttile[:], ttile[:], rqb_s[:], OP.mult),
                   wait=[("lns", L + 12 + 3 * oc + 1)], inc="lns")
            p.emit("vector", lambda e, oc=oc: e.tensor_tensor(
                z_s[:, oc * 512:(oc + 1) * 512], z_s[:, oc * 512:(oc + 1) * 512],
                ttile[:], OP.add),
                   wait=[("lns", L + 12 + 3 * oc + 2)], inc="lns")
        for oc in range(NCH):
            p.emit("vector", lambda e, oc=oc, j=oc % 2: e.tensor_tensor(
                sqz[j][:], z_s[:, oc * 512:(oc + 1) * 512],
                z_s[:, oc * 512:(oc + 1) * 512], OP.mult),
                   wait=[("lns", L + 12 + 3 * oc + 3), ("z2qm", 8 * qt + oc - 1)], inc="zsq")
            w = [("lns", L + 12 + 3 * oc + 3)]
            if oc == 0:
                w.append(("lns", (LNS_QT * (qt - 1) + 37) if qt > 0 else 0))
            p.emit("tensor", lambda e, oc=oc: e.matmul(
                PB[0:1, 1024:1536], ones128f_s[:], z_s[:, oc * 512:(oc + 1) * 512],
                start=(oc == 0), stop=(oc == NCH - 1)),
                   wait=w, inc="z2y" if oc == NCH - 1 else None)
            w = [("zsq", 8 * qt + oc + 1)]
            if oc == 0:
                w.append(("lns", (LNS_QT * (qt - 1) + 38) if qt > 0 else 0))
            p.emit("tensor", lambda e, oc=oc, j=oc % 2: e.matmul(
                PA[0:1, 1536:2048], ones128b_s[:], sqz[j][:],
                start=(oc == 0), stop=(oc == NCH - 1)),
                   wait=w, inc="z2qm" if True else None)
        p.emit("vector", lambda e: e.tensor_scalar(mbq_s[0:1, 0:512], PB[0:1, 1024:1536],
                                                   1.0 / D, None, OP.mult),
               wait=[("z2y", qt + 1), ("lns", L + 36)], inc="lns")
        p.emit("vector", lambda e: e.tensor_scalar(rqb_s[0:1, 0:512], PA[0:1, 1536:2048],
                                                   1.0 / D, None, OP.mult),
               wait=[("z2qm", 8 * (qt + 1)), ("lns", L + 37)], inc="lns")
        p.emit("vector", lambda e: e.tensor_tensor(ttile[0:1, 0:512], mbq_s[0:1, 0:512],
                                                   mbq_s[0:1, 0:512], OP.mult),
               wait=[("lns", L + 38)], inc="lns")
        p.emit("vector", lambda e: e.tensor_tensor(rqb_s[0:1, 0:512], rqb_s[0:1, 0:512],
                                                   ttile[0:1, 0:512], OP.subtract),
               wait=[("lns", L + 39)], inc="lns")
        p.emit("vector", lambda e: e.tensor_scalar(rqb_s[0:1, 0:512], rqb_s[0:1, 0:512],
                                                   EPS_LN, None, OP.add),
               wait=[("lns", L + 40)], inc="lns")
        p.emit("scalar", lambda e: e.activation(ttile[0:1, 0:512], rqb_s[0:1, 0:512], ACT.Ln),
               wait=[("lns", L + 41)], inc="lns")
        p.emit("scalar", lambda e: e.activation(rqb_s[0:1, 0:512], ttile[0:1, 0:512],
                                                ACT.Exp, scale=-0.5),
               wait=[("lns", L + 42)], inc="lns")
        p.emit("tensor", lambda e: e.matmul(PB[:, 1024:1536], ones1f_s[:], mbq_s[0:1, 0:512],
                                            start=True, stop=True),
               wait=[("lns", L + 43)], inc="lns")
        p.emit("vector", lambda e: e.tensor_copy(mbq_s[:], PB[:, 1024:1536]),
               wait=[("lns", L + 44)], inc="lns")
        p.emit("tensor", lambda e: e.matmul(PB[:, 1024:1536], ones1f_s[:], rqb_s[0:1, 0:512],
                                            start=True, stop=True),
               wait=[("lns", L + 45)], inc="lns")
        p.emit("vector", lambda e: e.tensor_copy(rqb_s[:], PB[:, 1024:1536]),
               wait=[("lns", L + 46)], inc="lns")
        for oc in range(NCH):
            j = oc % 2
            p.emit("vector", lambda e, oc=oc, j=j: e.tensor_tensor(
                outst[j][:], z_s[:, oc * 512:(oc + 1) * 512], mbq_s[:], OP.subtract),
                   wait=[("lns", L + 47 + 2 * oc), ("outd", 16 * (8 * qt + oc - 1))], inc="lns")
            p.emit("vector", lambda e, oc=oc, j=j: e.tensor_tensor(
                outst[j][:], outst[j][:], rqb_s[:], OP.mult),
                   wait=[("lns", L + 47 + 2 * oc + 1)], inc="lns")
            p.emit("sync", lambda e, oc=oc, qt=qt, j=j: ld(
                e, outv[oc][:, qt * 512:(qt + 1) * 512], outst[j][:]),
                   wait=[("lns", L + 47 + 2 * oc + 2)], inc="outd", inc_n=16)

    tail_waits = [("outd", 16 * 32)]
    if p.n("dbgs"):
        tail_waits.append(("dbgs", p.n("dbgs")))
    p.emit("gpsimd", lambda e: None, wait=tail_waits, inc="tail")

    return nc, p, {}, dbg


def finalize(nc, p):
    with nc.Block() as block:
        @block.sync
        def _(e):
            for fn in p.q["sync"]:
                fn(e)

        @block.tensor
        def _(e):
            for fn in p.q["tensor"]:
                fn(e)

        @block.vector
        def _(e):
            for fn in p.q["vector"]:
                fn(e)

        @block.scalar
        def _(e):
            for fn in p.q["scalar"]:
                fn(e)

        @block.gpsimd
        def _(e):
            for fn in p.q["gpsimd"]:
                fn(e)
    return nc


def shard_inputs(inputs):
    x = np.asarray(inputs["L0_attn_x"], dtype=np.float32)
    cos = np.asarray(inputs["L0_attn_rope_cos"], dtype=np.float32)
    sin = np.asarray(inputs["L0_attn_rope_sin"], dtype=np.float32)
    P = cos.shape[0]
    cos_pad = np.concatenate([np.ones((S - P, 32), np.float32), cos], axis=0)
    sin_pad = np.concatenate([np.zeros((S - P, 32), np.float32), sin], axis=0)
    pair_idx = (np.arange(128) % 64) // 2
    signv = np.where(np.arange(128) % 2 == 0, -1.0, 1.0).astype(np.float32)[:, None]
    qn = np.asarray(inputs["L0_attn_q_norm_w"], np.float32)
    kn = np.asarray(inputs["L0_attn_k_norm_w"], np.float32)
    qnv = np.concatenate([qn, qn])[:, None].astype(np.float32)
    knv = np.concatenate([kn, kn])[:, None].astype(np.float32)
    e64 = np.zeros((128, 2), np.float32); e64[0:64, 0] = 1; e64[64:128, 1] = 1
    bc2 = np.zeros((2, 128), np.float32); bc2[0, 0:64] = 1; bc2[1, 64:128] = 1
    swapT = np.zeros((128, 128), np.float32)
    for i_ in range(128):
        swapT[i_ ^ 1, i_] = 1.0
    g = np.zeros((128, 128), np.float32)
    g[0, 0:64] = 1; g[64, 0:64] = 1; g[32, 64:128] = 1; g[96, 64:128] = 1
    wq = np.asarray(inputs["L0_attn_wq"], np.float32)
    wk = np.asarray(inputs["L0_attn_wk"], np.float32)
    wv = np.asarray(inputs["L0_attn_wv"], np.float32)
    wo = np.asarray(inputs["L0_attn_wo"], np.float32)
    wg = np.asarray(inputs["L0_ffn_w_gate"], np.float32)
    wu = np.asarray(inputs["L0_ffn_w_up"], np.float32)
    wd = np.asarray(inputs["L0_ffn_w_down"], np.float32)
    common = {
        "wqT": np.ascontiguousarray(wq.T), "wkT": np.ascontiguousarray(wk.T),
        "wvT": np.ascontiguousarray(wv.T), "woT": np.ascontiguousarray(wo.T),
        "signv": signv, "qnv": qnv, "knv": knv,
        "e64": _bf(e64), "bc2": bc2, "swapT": _bf(swapT),
        "ones32": _bf(np.ones((128, 32), np.float32)), "g128": g,
        "ones1f": np.ones((1, 128), np.float32),
        "ones128f": np.ones((128, 1), np.float32),
        "ones128b": _bf(np.ones((128, 1), np.float32)),
        "clipx": np.asarray(inputs["L0_ffn_clip_x"], np.float32).reshape(1, 1),
        "clipd": np.asarray(inputs["L0_ffn_clip_down"], np.float32).reshape(1, 1),
        "wgT": np.ascontiguousarray(wg.T), "wuT": np.ascontiguousarray(wu.T),
        "wdT": np.ascontiguousarray(wd.T),
        "wg": wg, "wu": wu, "wd": wd,
    }
    in_maps = []
    for c in range(8):
        b = c // 4
        t0 = (c % 4) * T
        m = dict(common)
        m["xT"] = np.ascontiguousarray(x[b].T[:, t0:t0 + T])
        m["cosT"] = np.ascontiguousarray(cos_pad[t0:t0 + T, pair_idx].T)
        m["sinT"] = np.ascontiguousarray(sin_pad[t0:t0 + T, pair_idx].T)
        in_maps.append(m)
    return in_maps


def kernel(**inputs):
    in_maps = shard_inputs(inputs)
    nc, p, dram, dbg = build()
    finalize(nc, p)
    res = run_bass_kernel_spmd(nc, in_maps, core_ids=list(range(8)))
    out = np.empty((2, S, D), dtype=np.float32)
    for c in range(8):
        b = c // 4
        t0 = (c % 4) * T
        out[b, t0:t0 + T, :] = res.results[c]["outT"].T
    return out
